# version 08
# baseline (speedup 1.0000x reference)
from contextlib import ExitStack
import math
import numpy as np
import concourse.bass as bass
import concourse.mybir as mybir
from concourse.bass_utils import run_bass_kernel_spmd

F32 = mybir.dt.float32
BF16 = mybir.dt.bfloat16
I32 = mybir.dt.int32
AF = mybir.ActivationFunctionType
ALU = mybir.AluOpType
AX = mybir.AxisListType

B, L, H = 2, 8192, 1024
NCORE = 8
CORES_PER_SEQ = 4
OWN = L // CORES_PER_SEQ
CT = 512
NOWN = OWN // CT
PRE = OWN * (CORES_PER_SEQ - 1)
NPRE = PRE // CT
NTILE = NPRE + NOWN
E = 64
IM = 512
ISH = 2048
NSH = ISH // IM
NUNIT = E + NSH
EG = 4
GAMMA = 31.0 / 32.0
LNG = math.log(GAMMA)
SCALE_BASE = 512.0
MOE_SCALE = 2.5
EPS = 1e-5
TWO_PI = 2.0 * math.pi
INV_2PI = 1.0 / (2.0 * math.pi)
PI_HALF = math.pi / 2.0
CLS = 0.49999994
NEGBIG = -1.0e30
PB = H // 128
KC = PB
IBLK = IM // 128
ICH = IM // 128
HB = PB
TB = CT // 128

ENGINES = ("tensor", "vector", "scalar", "gpsimd", "sync")


class _Op:
    __slots__ = ("engine", "fn", "deps", "need_inc", "inc_val", "dma", "tag")

    def __init__(self, engine, fn, dma, tag):
        self.engine, self.fn, self.dma, self.tag = engine, fn, dma, tag
        self.deps = []
        self.need_inc = False
        self.inc_val = None


class Sched:
    def __init__(self):
        self.per_engine = {e: [] for e in ENGINES}
        self.res = {}
        self.ops = []

    def op(self, engine, fn, reads=(), writes=(), dma=None, tag=""):
        o = _Op(engine, fn, dma, tag)
        deps = []
        for r in reads:
            st = self.res.get(r)
            if st and st["w"] is not None:
                deps.append(st["w"])
        for w in writes:
            st = self.res.get(w)
            if st:
                if st["w"] is not None:
                    deps.append(st["w"])
                deps.extend(st["r"])
        seen = set()
        for d in deps:
            if id(d) in seen:
                continue
            seen.add(id(d))
            if (not d.dma) and d.engine == engine == "tensor":
                continue
            o.deps.append(d)
        for r in reads:
            self.res.setdefault(r, {"w": None, "r": []})["r"].append(o)
        for w in writes:
            self.res[w] = {"w": o, "r": []}
        self.ops.append(o)
        self.per_engine[engine].append(o)
        return o

    def build(self, block, sems, waitall=()):
        for o in self.ops:
            for d in o.deps:
                d.need_inc = True
        cum = {}

        def semkey(o):
            return o.dma if o.dma else o.engine

        for e in ENGINES:
            for o in self.per_engine[e]:
                if o.dma:
                    o.need_inc = True
                if o.need_inc:
                    k = semkey(o)
                    cum[k] = cum.get(k, 0) + (16 if o.dma else 1)
                    o.inc_val = cum[k]
        for e in ENGINES:
            for o in self.per_engine[e]:
                if o.dma in waitall:
                    o.inc_val = cum[o.dma]

        def emit_for(engine, eng):
            waited = {}
            for o in self.per_engine[engine]:
                for d in o.deps:
                    k = semkey(d)
                    v = d.inc_val
                    if waited.get(k, 0) >= v:
                        continue
                    eng.wait_ge(sems[k], v)
                    waited[k] = v
                ins = o.fn(eng)
                if o.need_inc:
                    assert ins is not None, f"op {o.tag} returned no instruction"
                    ins.then_inc(sems[semkey(o)], 16 if o.dma else 1)

        block.tensor(lambda eng: emit_for("tensor", eng))
        block.vector(lambda eng: emit_for("vector", eng))
        block.scalar(lambda eng: emit_for("scalar", eng))
        block.gpsimd(lambda eng: emit_for("gpsimd", eng))
        block.sync(lambda eng: emit_for("sync", eng))
        return cum


def bc_ap(sbt, n_p, rep, inner, off=0):
    return bass.AP(sbt.ap().tensor, off, [[inner, n_p], [0, rep], [1, inner]])


def build_program(debug=False):
    nc = bass.Bass("TRN2", target_bir_lowering=False)
    dr = lambda name, shape, dt, kind: nc.dram_tensor(name, shape, dt, kind=kind).ap()
    x_hbm = dr("x_hbm", [NTILE, 128, PB * CT], F32, "ExternalInput")
    wq_hbm = dr("wq_hbm", [128, KC * H], F32, "ExternalInput")
    wk_hbm = dr("wk_hbm", [128, KC * H], F32, "ExternalInput")
    wv_hbm = dr("wv_hbm", [128, KC * H], F32, "ExternalInput")
    router_hbm = dr("router_hbm", [128, KC * E], F32, "ExternalInput")
    bias_hbm = dr("bias_hbm", [1, E], F32, "ExternalInput")
    up_hbm = dr("up_hbm", [NUNIT, 128, KC * IM], F32, "ExternalInput")
    down_hbm = dr("down_hbm", [NUNIT, 128, ICH * H], F32, "ExternalInput")
    base_hbm = dr("base_hbm", [1, 1], F32, "ExternalInput")
    pair_hbm = dr("pair_hbm", [1, H], F32, "ExternalInput")
    altc_hbm = dr("altc_hbm", [128, 1], F32, "ExternalInput")
    out_hbm = dr("out_hbm", [NOWN, 128, PB * CT], F32, "ExternalOutput")

    sch = Sched()
    es = ExitStack()
    with es:
        def sb(n, sh, dt):
            return es.enter_context(nc.sbuf_tensor(n, sh, dt))

        def psb(n, sh):
            return es.enter_context(nc.psum_tensor(n, sh, F32))

        wv_sb = sb("wv_sb", [128, KC, H], BF16)
        router_sb = sb("router_sb", [128, KC, E], BF16)
        d_sb = sb("d_sb", [128, TB, CT], BF16)
        ident_bf = sb("ident_bf", [128, 128], BF16)
        gcol = sb("gcol", [128, TB], F32)
        growbc = sb("growbc", [128, CT], BF16)
        altcol = sb("altcol", [128, 1], F32)
        bias_bc = sb("bias_bc", [128, E], F32)
        negbig_col = sb("negbig_col", [128, 1], F32)
        pih = sb("pih", [128, 1], F32)
        b2048 = sb("b2048", [128, 1], F32)
        ctlng = sb("ctlng", [128, 1], F32)
        iota_row = sb("iota_row", [1, CT], F32)
        invf_row = sb("invf_row", [1, H], F32)
        lnsc_row = sb("lnsc_row", [1, H], F32)
        rows = sb("rows", [1, 4 * CT], F32)
        hrow = rows[0:1, 0:H]
        hrow2 = rows[0:1, H:2 * H]
        base_sb = sb("base_sb", [1, 1], F32)
        pos_row = sb("pos_row", [1, CT], F32)
        bias_row = sb("bias_row", [1, E], F32)
        growrow = sb("growrow", [1, CT], F32)
        onesf_row = sb("onesf_row", [1, 128], F32)
        _x1 = sb("x_sb", [128, PB, CT], BF16)
        x_sb = [_x1, _x1]
        xn_sb = sb("xn_sb", [128, PB, CT], BF16)
        wb1 = sb("wb1", [128, PB, CT], BF16)
        sqb = sb("sqb", [128, IBLK, CT], BF16)
        sct_sb = sqb
        qx_sb = sb("qx_sb", [128, PB, CT], BF16)
        vtm_sb = sb("vtm_sb", [128, TB, H], BF16)
        tabC = sb("tabC", [128, PB, CT], BF16)
        tabS = sb("tabS", [128, PB, CT], BF16)
        s_sb = sb("s_sb", [128, PB, H], BF16)
        acc_sb = sb("acc_sb", [128, PB, CT], F32)
        c_sb = [sb(f"c_sb{i}", [128, CT], BF16) for i in range(2)]
        unA = sb("unA", [128, PB, CT], BF16)
        unB = sb("unB", [128, TB, H], BF16)
        wslot = [sb(f"wslot{i}", [128, KC * IM], BF16) for i in range(4)]
        tg_u = sb("tg_u", [128, CT], F32)
        tg_ni = sb("tg_ni", [128, CT], I32)
        tg_nf = sb("tg_nf", [128, CT], F32)
        tg_ds = sb("tg_ds", [128, CT], F32)
        tg_dc = sb("tg_dc", [128, CT], F32)
        tg_sc = sb("tg_sc", [128, CT], F32)
        rt_sc = sb("rt_sc", [128, TB * E], F32)
        rt_sfc = sb("rt_sfc", [128, TB * E], F32)
        rt_m = sb("rt_m", [128, TB * E], F32)
        rt_sel = sb("rt_sel", [128, TB * E], F32)
        rt_eq = sb("rt_eq", [128, TB * E], F32)
        rt_mb = sb("rt_mb", [128, TB * E], BF16)
        rt_gs = sb("rt_gs", [128, TB * PB], F32)
        rt_gs0 = sb("rt_gs0", [128, TB * PB], F32)
        rt_g1 = sb("rt_g1", [128, TB * PB], F32)
        rt_r1 = sb("rt_r1", [128, TB], F32)
        rt_r2 = sb("rt_r2", [128, TB], F32)

        ps0 = psb("ps0", [128, 512])
        ps1 = psb("ps1", [128, 512])
        psS = psb("psS", [128, 1024])
        psA = [psb(f"psA{i}", [128, 512]) for i in range(4)]
        dn_banks = [ps0[:], ps1[:], psS[:, 0:512], psS[:, 512:1024],
                    psA[0][:], psA[1][:], psA[2][:], psA[3][:]]
        dn_keys = [("psb", 0), ("psb", 1), "psS", "psS",
                   ("psA", 0), ("psA", 1), ("psA", 2), ("psA", 3)]

        sems = {}
        for e in ENGINES:
            sems[e] = es.enter_context(nc.semaphore("sem_" + e))
        for k in ("dst", "dx", "dw0", "dw1", "dw2", "dw3", "dout", "dtap"):
            sems[k] = es.enter_context(nc.semaphore("sem_" + k))

        OP = sch.op
        dbg_n = [0]

        def tap(name, ap, reads, dt=BF16):
            if not debug:
                return
            shp = list(ap.shape)
            d = nc.dram_tensor(f"dbg_{name}", shp, dt, kind="ExternalOutput").ap()

            def f(sy, ap=ap, d=d):
                return sy.dma_start(out=d, in_=ap)
            OP("sync", f, reads=list(reads), writes=[f"dbgw_{name}"], dma="dtap",
               tag=f"tap_{name}")
            dbg_n[0] += 1

        ones_bf_ap = nc.const_aps.tensor(1.0, [128, 1], BF16)
        nc.const_aps.register(F32, 2048.0, b2048.ap())
        nc.const_aps.register(F32, PI_HALF, pih.ap())
        nc.const_aps.register(F32, CT * LNG, ctlng.ap())

        def g_setup(g):
            g.memset(onesf_row[:], 1.0)
            g.memset(negbig_col[:], NEGBIG)
            g.memset(pih[:], PI_HALF)
            g.memset(b2048[:], 2048.0)
            g.memset(ctlng[:], CT * LNG)
            g.iota(iota_row[:], pattern=[[1, CT]], channel_multiplier=0,
                   allow_small_or_imprecise_dtypes=True)
            g.iota(tg_ds[:, 0:128], pattern=[[-1, 128]], channel_multiplier=1,
                   allow_small_or_imprecise_dtypes=True)
            return g.iota(gcol[:], pattern=[[128, TB]], channel_multiplier=1,
                          allow_small_or_imprecise_dtypes=True)
        OP("gpsimd", g_setup, writes=["consts", "gcol_i",
                                      "tg_u", "tg_ni", "tg_nf", "tg_ds", "rows"], tag="setup_g")

        def g_id1(g):
            return g.tensor_scalar(tg_ds[:, 0:128], tg_ds[:, 0:128], 0.0, None,
                                   ALU.is_equal)
        OP("gpsimd", g_id1, reads=["tg_ds"], writes=["tg_ds"], tag="setup_id1")

        def g_id2(g):
            return g.tensor_copy(ident_bf[:], tg_ds[:, 0:128])
        OP("gpsimd", g_id2, reads=["tg_ds"], writes=["consts"], tag="setup_id2")

        def mk_dma(dst, src, wkey, tag):
            def f(g):
                return g.dma_start(out=dst, in_=src)
            OP("gpsimd", f, writes=[wkey], dma="dst", tag=tag)

        mk_dma(hrow2, pair_hbm, "featrows_i", "dma_pair")
        mk_dma(altcol[:], altc_hbm, "alt", "dma_alt")

        def s_gcol(s):
            return s.activation(gcol[:], gcol[:], AF.Exp, scale=-LNG, bias=CT * LNG)
        OP("scalar", s_gcol, reads=["gcol_i", "consts"], writes=["gcol"], tag="setup_gcol")



        def v_grow(v):
            return v.tensor_scalar(growrow[:], iota_row[:], LNG, None, ALU.mult)
        OP("vector", v_grow, reads=["consts"], writes=["growr"], tag="setup_growr1")

        def s_grow(s):
            return s.activation(growrow[:], growrow[:], AF.Exp)
        OP("scalar", s_grow, reads=["growr"], writes=["growr2"], tag="setup_growr2")

        def mm_growbc(tn):
            return tn.matmul(ps0[:, 0:CT], onesf_row[:], growrow[:],
                             start=True, stop=True)
        OP("tensor", mm_growbc, reads=["growr2", "consts"], writes=[("psb", 0)],
           tag="setup_growbc1")

        def v_growbc(v):
            return v.tensor_copy(growbc[:], ps0[:, 0:CT])
        OP("vector", v_growbc, reads=[("psb", 0)], writes=["growbc"], tag="setup_growbc2")

        def v_featv(v):
            v.tensor_scalar(invf_row[:], hrow2, -math.log(10000.0) / (H // 2),
                            None, ALU.mult)
            return v.tensor_scalar(lnsc_row[:], hrow2, 2.0, 0.4 * H, ALU.mult,
                                   ALU.add)
        OP("vector", v_featv, reads=["featrows_i", "rows", "consts"], writes=["featrows_v"],
           tag="setup_featv")

        def v_featv2(v):
            return v.tensor_scalar(lnsc_row[:], lnsc_row[:], 1.0 / (1.4 * H), None,
                                   ALU.mult)
        OP("vector", v_featv2, reads=["featrows_v"], writes=["featrows_v"],
           tag="setup_featv2")

        def s_feat(s):
            s.activation(invf_row[:], invf_row[:], AF.Exp)
            return s.activation(lnsc_row[:], lnsc_row[:], AF.Ln)
        OP("scalar", s_feat, reads=["featrows_v"], writes=["featrows_s"], tag="setup_feats")

        def v_featf(v):
            return v.tensor_scalar(lnsc_row[:], lnsc_row[:], 1.0 / SCALE_BASE, None,
                                   ALU.mult)
        OP("vector", v_featf, reads=["featrows_s"], writes=["featrows"], tag="setup_featf")

        mk_dma(wv_sb[:], wv_hbm, "wv", "dma_wv")
        mk_dma(router_sb[:], router_hbm, "router", "dma_router")
        mk_dma(bias_row[:], bias_hbm, "bias_r", "dma_bias")
        mk_dma(base_sb[:], base_hbm, "base", "dma_base")

        def mm_biasbc(tn):
            return tn.matmul(ps1[:, 0:E], onesf_row[:], bias_row[:],
                             start=True, stop=True)
        OP("tensor", mm_biasbc, reads=["bias_r", "consts"], writes=[("psb", 1)],
           tag="setup_biasbc1")

        def s_biasbc(s):
            return s.copy(bias_bc[:], ps1[:, 0:E])
        OP("scalar", s_biasbc, reads=[("psb", 1)], writes=["bias_bc"], tag="setup_biasbc2")

        for kb in range(TB):
            def g_d(g, kb=kb):
                return g.iota(tg_u[:], pattern=[[1, CT]], channel_multiplier=-1,
                              base=-kb * 128, allow_small_or_imprecise_dtypes=True)
            OP("gpsimd", g_d, writes=["tg_u"], tag=f"dgen{kb}")

            def s_d(s):
                return s.activation(tg_ds[:], tg_u[:], AF.Exp, scale=LNG)
            OP("scalar", s_d, reads=["tg_u"], writes=["tg_ds"], tag=f"dexp{kb}")

            def v_d1(v):
                return v.tensor_scalar(tg_dc[:], tg_u[:], 0.0, None, ALU.is_ge)
            OP("vector", v_d1, reads=["tg_u"], writes=["tg_dc"], tag=f"dge{kb}")

            def v_d2(v, kb=kb):
                return v.tensor_tensor(d_sb[:, kb, :], tg_ds[:], tg_dc[:], ALU.mult)
            OP("vector", v_d2, reads=["tg_ds", "tg_dc"], writes=[("dmask", kb)],
               tag=f"dmask{kb}")

        def v_zs(v):
            return v.memset(s_sb[:], 0.0)
        OP("vector", v_zs, writes=["S"], tag="zeroS")

        def load_x(t):
            def f(g, t=t):
                return g.dma_start(out=x_sb[t % 2][:], in_=x_hbm[t])
            OP("gpsimd", f, writes=["x"], dma="dx", tag=f"ldx{t}")

        def pos_tile(t):
            def f(v, t=t):
                return v.tensor_scalar(pos_row[:], iota_row[:], base_sb[0:1, 0:1],
                                       float(t * CT), ALU.add, ALU.add)
            OP("vector", f, reads=["base", "consts"], writes=["pos_row"], tag=f"pos{t}")

        def ln_tile(t):
            xb = x_sb[t % 2]

            def mm_sum(tn, xb=xb):
                last = None
                for pb_ in range(PB):
                    last = tn.matmul(ps0[0:1, 0:CT], ones_bf_ap, xb[:, pb_, :],
                                     start=(pb_ == 0), stop=(pb_ == PB - 1))
                return last
            OP("tensor", mm_sum, reads=["x"], writes=[("psb", 0)], tag=f"lnsum{t}")

            def s_sq(s, xb=xb):
                return s.activation(wb1[:], xb[:], AF.Square)
            OP("scalar", s_sq, reads=["x"], writes=["wb1"], tag=f"lnsq{t}")

            def mm_sq(tn):
                last = None
                for pb_ in range(PB):
                    last = tn.matmul(ps1[0:1, 0:CT], ones_bf_ap, wb1[:, pb_, :],
                                     start=(pb_ == 0), stop=(pb_ == PB - 1))
                return last
            OP("tensor", mm_sq, reads=["wb1"], writes=[("psb", 1)], tag=f"lnsum2_{t}")

            MEAN, WORK, RSTD, MR = 0, 1, 2, 3
            rr = lambda i: rows[0:1, i * CT:(i + 1) * CT]

            def v_rows(v):
                v.tensor_scalar(rr(MEAN), ps0[0:1, 0:CT], 1.0 / H, None, ALU.mult)
                return v.tensor_scalar(rr(WORK), ps1[0:1, 0:CT], 1.0 / H, None,
                                       ALU.mult)
            OP("vector", v_rows, reads=[("psb", 0), ("psb", 1)], writes=["rows"],
               tag=f"lnrows{t}")

            def v_rows2(v):
                return v.tensor_tensor(rr(RSTD), rr(MEAN), rr(MEAN), ALU.mult)
            OP("vector", v_rows2, reads=["rows"], writes=["rows"], tag=f"lnrsq{t}")

            def v_rows3(v):
                return v.scalar_tensor_tensor(rr(WORK), rr(WORK), EPS, rr(RSTD),
                                              ALU.add, ALU.subtract)
            OP("vector", v_rows3, reads=["rows"], writes=["rows"], tag=f"lnvar{t}")

            def s_rstd(s):
                return s.activation(rr(WORK), rr(WORK), AF.Ln)
            OP("scalar", s_rstd, reads=["rows"], writes=["rows"], tag=f"lnrstd{t}")

            def s_rstd2(s):
                return s.activation(rr(RSTD), rr(WORK), AF.Exp, scale=-0.5)
            OP("scalar", s_rstd2, reads=["rows"], writes=["rows"], tag=f"lnrstd2{t}")

            def v_mr(v):
                return v.tensor_tensor(rr(MR), rr(MEAN), rr(RSTD), ALU.mult)
            OP("vector", v_mr, reads=["rows"], writes=["rows"], tag=f"lnmr{t}")

            def mm_bc(tn):
                tn.matmul(ps0[:, 0:CT], onesf_row[:], rr(RSTD), start=True, stop=True)
                return tn.matmul(ps1[:, 0:CT], onesf_row[:], rr(MR), start=True,
                                 stop=True)
            OP("tensor", mm_bc, reads=["rows", "consts"],
               writes=[("psb", 0), ("psb", 1)], tag=f"lnbc{t}")

            rb = bass.AP(ps0.ap().tensor, 0, [[512, 128], [0, PB], [1, CT]])
            mb = bass.AP(ps1.ap().tensor, 0, [[512, 128], [0, PB], [1, CT]])

            def v_xn(v, xb=xb, rb=rb):
                return v.tensor_tensor(xn_sb[:], xb[:], rb, ALU.mult)
            OP("vector", v_xn, reads=["x", ("psb", 0)], writes=["xn"], tag=f"xn{t}")

            def v_xn2(v, mb=mb):
                return v.tensor_tensor(xn_sb[:], xn_sb[:], mb, ALU.subtract)
            OP("vector", v_xn2, reads=["xn", ("psb", 1)], writes=["xn"], tag=f"xn2{t}")

        def sin_tables(t):
            for pbi in range(PB):
                def mm_th(tn, pbi=pbi):
                    return tn.matmul([ps0, ps1][pbi % 2][:, 0:CT],
                                     invf_row[0:1, pbi * 128:(pbi + 1) * 128],
                                     pos_row[:], start=True, stop=True)
                OP("tensor", mm_th, reads=["pos_row", "featrows"],
                   writes=[("psb", pbi % 2)], tag=f"th{t}_{pbi}")

                def s_u(s, pbi=pbi):
                    return s.activation(tg_u[:], [ps0, ps1][pbi % 2][:, 0:CT], AF.Copy,
                                        scale=INV_2PI, bias=2048.0)
                OP("scalar", s_u, reads=[("psb", pbi % 2), "consts"], writes=["tg_u"],
                   tag=f"tu{t}_{pbi}")

                def g_cast1(g):
                    return g.tensor_copy(tg_ni[:], tg_u[:])
                OP("gpsimd", g_cast1, reads=["tg_u"], writes=["tg_ni"],
                   tag=f"tc1{t}_{pbi}")

                def g_cast2(g):
                    return g.tensor_copy(tg_nf[:], tg_ni[:])
                OP("gpsimd", g_cast2, reads=["tg_ni"], writes=["tg_nf"],
                   tag=f"tc2{t}_{pbi}")

                fold_steps = [
                    (lambda v: v.tensor_tensor(tg_ds[:], tg_u[:], tg_nf[:],
                                               ALU.subtract),
                     ["tg_nf", "tg_u"], ["tg_ds"]),
                    (lambda v: v.tensor_scalar(tg_dc[:], tg_ds[:], 0.5, None,
                                               ALU.is_ge),
                     ["tg_ds"], ["tg_dc"]),
                    (lambda v: v.scalar_tensor_tensor(tg_ds[:], tg_dc[:], -1.0,
                                                      tg_ds[:], ALU.mult, ALU.add),
                     ["tg_dc", "tg_ds"], ["tg_ds"]),
                    (lambda v: v.tensor_scalar(tg_ds[:], tg_ds[:], -CLS, CLS,
                                               ALU.max, ALU.min),
                     ["tg_ds"], ["tg_ds"]),
                    (lambda v: v.tensor_scalar(tg_dc[:], tg_ds[:], 0.25, None,
                                               ALU.is_gt),
                     ["tg_ds"], ["tg_dc"]),
                    (lambda v: v.scalar_tensor_tensor(tg_dc[:], tg_dc[:], -1.0,
                                                      tg_ds[:], ALU.mult, ALU.add),
                     ["tg_dc", "tg_ds"], ["tg_dc"]),
                    (lambda v: v.tensor_scalar(tg_dc[:], tg_dc[:], -0.74999994,
                                               0.24999994, ALU.max, ALU.min),
                     ["tg_dc"], ["tg_dc"]),
                ]
                for fi, (ffn_, frd, fwr) in enumerate(fold_steps):
                    OP("vector", ffn_, reads=frd, writes=fwr, tag=f"tf{t}_{pbi}_{fi}")

                def s_sincos(s, pbi=pbi):
                    s.activation(tabS[:, pbi, :], tg_ds[:], AF.Sin, scale=TWO_PI)
                    return s.activation(tabC[:, pbi, :], tg_dc[:], AF.Sin, scale=TWO_PI,
                                        bias=PI_HALF)
                OP("scalar", s_sincos, reads=["tg_ds", "tg_dc", "consts"],
                   writes=["tabS", "tabC"], tag=f"tsin{t}_{pbi}")

                def v_alt(v, pbi=pbi):
                    return v.tensor_scalar(tabS[:, pbi, :], tabS[:, pbi, :], altcol[:],
                                           None, ALU.mult)
                OP("vector", v_alt, reads=["alt", "tabS"], writes=["tabS"],
                   tag=f"talt{t}_{pbi}")

        def xpos_apply(t, sgn, buf_sb, buf_key):
            mask = [i ^ 1 for i in range(32)]

            def v_sh(v, buf_sb=buf_sb):
                return v.stream_shuffle(wb1[:].rearrange("p a b -> p (a b)"),
                                        buf_sb[:].rearrange("p a b -> p (a b)"), mask)
            OP("vector", v_sh, reads=[buf_key], writes=["wb1"], tag=f"xsh{t}{buf_key}")

            for pbi in range(PB):
                def mm_ths(tn, pbi=pbi):
                    return tn.matmul([ps0, ps1][pbi % 2][:, 0:CT],
                                     lnsc_row[0:1, pbi * 128:(pbi + 1) * 128],
                                     pos_row[:], start=True, stop=True)
                OP("tensor", mm_ths, reads=["pos_row", "featrows"],
                   writes=[("psb", pbi % 2)], tag=f"ths{t}{buf_key}{pbi}")

                def s_sc(s, sgn=sgn, pbi=pbi):
                    return s.activation(tg_sc[:], [ps0, ps1][pbi % 2][:, 0:CT], AF.Exp,
                                        scale=sgn)
                OP("scalar", s_sc, reads=[("psb", pbi % 2)], writes=["tg_sc"],
                   tag=f"xsc{t}{buf_key}{pbi}")

                def g_t2(g, pbi=pbi):
                    return g.tensor_tensor(sqb[:, 0, :], wb1[:, pbi, :], tabS[:, pbi, :],
                                           ALU.mult)
                OP("gpsimd", g_t2, reads=["wb1", "tabS"], writes=["sqb"],
                   tag=f"xt2{t}{buf_key}{pbi}")

                def v_fin1(v, pbi=pbi, buf_sb=buf_sb):
                    return v.tensor_tensor(buf_sb[:, pbi, :], buf_sb[:, pbi, :],
                                           tabC[:, pbi, :], ALU.mult)
                OP("vector", v_fin1, reads=["tabC", buf_key], writes=[buf_key],
                   tag=f"xf1{t}{buf_key}{pbi}")

                def v_fin2(v, pbi=pbi, buf_sb=buf_sb):
                    return v.tensor_tensor(buf_sb[:, pbi, :], buf_sb[:, pbi, :],
                                           sqb[:, 0, :], ALU.add)
                OP("vector", v_fin2, reads=["sqb", buf_key], writes=[buf_key, "sqb"],
                   tag=f"xf2{t}{buf_key}{pbi}")

                def v_fin3(v, pbi=pbi, buf_sb=buf_sb):
                    return v.tensor_tensor(buf_sb[:, pbi, :], buf_sb[:, pbi, :],
                                           tg_sc[:], ALU.mult)
                OP("vector", v_fin3, reads=["tg_sc", buf_key], writes=[buf_key],
                   tag=f"xf3{t}{buf_key}{pbi}")

        def proj_fm(t, slot_lo, out_sb, out_key, tagp):
            for ob in range(PB):
                bank = [ps0, ps1][ob % 2]

                def mm(tn, ob=ob, bank=bank):
                    last = None
                    for kc in range(KC):
                        si = slot_lo if kc < KC // 2 else slot_lo + 1
                        wv_ = wslot[si][:].rearrange("p (k o) -> p k o", k=KC // 2)
                        wap = wv_[:, kc % (KC // 2), ob * 128:(ob + 1) * 128]
                        last = tn.matmul(bank[:], wap, xn_sb[:, kc, :],
                                         start=(kc == 0), stop=(kc == KC - 1))
                    return last
                OP("tensor", mm, reads=["xn", ("wslot", slot_lo), ("wslot", slot_lo + 1)],
                   writes=[("psb", ob % 2)], tag=f"{tagp}mm{t}_{ob}")

                def ev(s, ob=ob, bank=bank, out_sb=out_sb):
                    return s.copy(out_sb[:, ob, :], bank[:])
                OP("scalar", ev, reads=[("psb", ob % 2)], writes=[out_key],
                   tag=f"{tagp}ev{t}_{ob}")

        def proj_tm_v(t):
            for tb in range(TB):
                for hf in range(2):
                    i = (tb * 2 + hf) % 2
                    bank = [ps0, ps1][i]

                    def mm(tn, tb=tb, hf=hf, bank=bank):
                        last = None
                        for kc in range(KC):
                            xap = xn_sb[:, kc, tb * 128:(tb + 1) * 128]
                            last = tn.matmul(bank[:], xap,
                                             wv_sb[:, kc, hf * 512:(hf + 1) * 512],
                                             start=(kc == 0), stop=(kc == KC - 1))
                        return last
                    OP("tensor", mm, reads=["xn", "wv"], writes=[("psb", i)],
                       tag=f"vmm{t}_{tb}{hf}")

                    def ev(s, tb=tb, hf=hf, bank=bank):
                        return s.copy(vtm_sb[:, tb, hf * 512:(hf + 1) * 512], bank[:])
                    OP("scalar", ev, reads=[("psb", i)], writes=["vtm"],
                       tag=f"vev{t}_{tb}{hf}")

        def k_transpose(t):
            for tb in range(TB):
                for ob in range(PB):
                    i = (tb * PB + ob) % 2
                    bank = [ps0, ps1][i]

                    def mm(tn, tb=tb, ob=ob, bank=bank):
                        return tn.matmul(bank[:, 0:128],
                                         unA[:, ob, tb * 128:(tb + 1) * 128],
                                         ident_bf[:], start=True, stop=True)
                    OP("tensor", mm, reads=["unA", "consts"], writes=[("psb", i)],
                       tag=f"ktr{t}_{tb}{ob}")

                    def ev(s, tb=tb, ob=ob, bank=bank):
                        return s.mul(unB[:, tb, ob * 128:(ob + 1) * 128], bank[:, 0:128],
                                     gcol[:, tb:tb + 1])
                    OP("scalar", ev, reads=[("psb", i), "gcol"], writes=["unB"],
                       tag=f"ktev{t}_{tb}{ob}")

        def state_update(t):
            for dkb in range(PB):
                def mm(tn, dkb=dkb):
                    last = None
                    for tc in range(TB):
                        kap = unB[:, tc, dkb * 128:(dkb + 1) * 128]
                        tn.matmul(psS[:, 0:512], kap, vtm_sb[:, tc, 0:512],
                                  start=(tc == 0), stop=(tc == TB - 1))
                        last = tn.matmul(psS[:, 512:1024], kap, vtm_sb[:, tc, 512:1024],
                                         start=(tc == 0), stop=(tc == TB - 1))
                    return last
                OP("tensor", mm, reads=["unB", "vtm"], writes=["psS"], tag=f"st{t}_{dkb}")

                def v_up(v, dkb=dkb):
                    return v.scalar_tensor_tensor(s_sb[:, dkb, :], s_sb[:, dkb, :],
                                                  GAMMA ** CT, psS[:], ALU.mult, ALU.add)
                OP("vector", v_up, reads=["psS"], writes=["S"], tag=f"sup{t}_{dkb}")

        def attn_tile(t):
            for kb in range(TB):
                bank = [ps0, ps1][kb % 2]

                def mm(tn, kb=kb, bank=bank):
                    last = None
                    for kc in range(KC):
                        kap = unA[:, kc, kb * 128:(kb + 1) * 128]
                        last = tn.matmul(bank[:], kap, qx_sb[:, kc, :],
                                         start=(kc == 0), stop=(kc == KC - 1))
                    return last
                OP("tensor", mm, reads=["unA", "qx"], writes=[("psb", kb % 2)],
                   tag=f"sct{t}_{kb}")

                def v_m(v, kb=kb, bank=bank):
                    return v.tensor_tensor(sct_sb[:, kb, :], bank[:], d_sb[:, kb, :],
                                           ALU.mult)
                OP("vector", v_m, reads=[("psb", kb % 2), ("dmask", kb)], writes=["sqb"],
                   tag=f"sctm{t}_{kb}")

            gb = bc_ap(growbc, 128, PB, CT)

            def v_qt(v):
                return v.tensor_tensor(qx_sb[:], qx_sb[:], gb, ALU.mult)
            OP("vector", v_qt, reads=["qx", "growbc"], writes=["qx"], tag=f"qt{t}")

            for dvb in range(PB):
                bank = psA[dvb % 4]

                def mm(tn, dvb=dvb, bank=bank):
                    last = None
                    for kb in range(TB):
                        vap = vtm_sb[:, kb, dvb * 128:(dvb + 1) * 128]
                        last = tn.matmul(bank[:], vap, sct_sb[:, kb, :],
                                         start=(kb == 0), stop=False)
                    for kc in range(KC):
                        sap = s_sb[:, kc, dvb * 128:(dvb + 1) * 128]
                        last = tn.matmul(bank[:], sap, qx_sb[:, kc, :],
                                         start=False, stop=(kc == KC - 1))
                    return last
                OP("tensor", mm, reads=["sqb", "vtm", "S", "qx"],
                   writes=[("psA", dvb % 4)], tag=f"attn{t}_{dvb}")

                def v_acc(v, dvb=dvb, bank=bank, t=t):
                    return v.tensor_tensor(acc_sb[:, dvb, :], x_sb[t % 2][:, dvb, :],
                                           bank[:], ALU.add)
                OP("vector", v_acc, reads=[("psA", dvb % 4), "x"],
                   writes=[("acc", dvb)], tag=f"acc{t}_{dvb}")

        def routing_tile(t):
            def mm(tn):
                last = None
                for tb in range(TB):
                    for kc in range(KC):
                        xap = xn_sb[:, kc, tb * 128:(tb + 1) * 128]
                        last = tn.matmul(ps0[:, tb * E:(tb + 1) * E], xap,
                                         router_sb[:, kc, :],
                                         start=(kc == 0), stop=(kc == KC - 1))
                return last
            OP("tensor", mm, reads=["xn", "router"], writes=[("psb", 0)], tag=f"rlg{t}")

            def s_ex(s):
                return s.activation(rt_eq[:], ps0[:, 0:TB * E], AF.Exp, scale=-1.0)
            OP("scalar", s_ex, reads=[("psb", 0)], writes=["rt_eq"], tag=f"rex{t}")

            sfc3 = rt_sfc[:].rearrange("p (g e) -> p g e", e=PB)
            eq3 = rt_eq[:].rearrange("p (g e) -> p g e", e=PB)
            m3g = rt_m[:].rearrange("p (g e) -> p g e", e=PB)
            gs3 = rt_gs[:].rearrange("p (tb g) -> p tb g", g=PB)
            gs03 = rt_gs0[:].rearrange("p (tb g) -> p tb g", g=PB)
            g13 = rt_g1[:].rearrange("p (tb g) -> p tb g", g=PB)
            m3t = rt_m[:].rearrange("p (tb e) -> p tb e", e=E)
            sel3 = rt_sel[:].rearrange("p (tb e) -> p tb e", e=E)
            eq3t = rt_eq[:].rearrange("p (tb e) -> p tb e", e=E)
            g1bg = bass.AP(rt_g1.ap().tensor, 0, [[TB * PB, 128], [1, TB * PB], [0, PB]])
            gsbg = bass.AP(rt_gs.ap().tensor, 0, [[TB * PB, 128], [1, TB * PB], [0, PB]])
            r1bg = bass.AP(rt_r1.ap().tensor, 0, [[TB, 128], [1, TB], [0, PB]])
            r1be = bass.AP(rt_r1.ap().tensor, 0, [[TB, 128], [1, TB], [0, E]])

            def v_sig1(v):
                return v.tensor_scalar(rt_eq[:], rt_eq[:], 1.0, None, ALU.add)
            OP("vector", v_sig1, reads=["rt_eq"], writes=["rt_eq"], tag=f"rsg1{t}")

            def v_sig2(v):
                return v.reciprocal(rt_sc[:], rt_eq[:])
            OP("vector", v_sig2, reads=["rt_eq"], writes=["rt_sc"], tag=f"rsg2{t}")

            def v_sig3(v):
                bb = bc_ap(bias_bc, 128, TB, E)
                return v.tensor_tensor(rt_sfc[:], rt_sc[:], bb, ALU.add)
            OP("vector", v_sig3, reads=["rt_sc", "bias_bc"], writes=["rt_sfc"],
               tag=f"rsg3{t}")

            def v_sig4(v):
                return v.reduce_max(rt_gs[:], sfc3, axis=AX.X)
            OP("vector", v_sig4, reads=["rt_sfc"], writes=["rt_gs"], tag=f"rsg4{t}")

            def g_b1(g):
                return g.tensor_copy(rt_g1[:], rt_gs[:])
            OP("gpsimd", g_b1, reads=["rt_gs"], writes=["rt_g1"], tag=f"rb1{t}")

            def v_eq1(v):
                v.tensor_tensor(eq3, sfc3, g1bg, ALU.is_equal)
                return v.nop() if False else v.tensor_copy(rt_sel[:], rt_sfc[:])
            OP("vector", v_eq1, reads=["rt_g1", "rt_sfc"], writes=["rt_eq", "rt_sel"],
               tag=f"req1{t}")

            def g_s1(g):
                return g.tensor_scalar(rt_eq[:], rt_eq[:], NEGBIG, None, ALU.mult)
            OP("gpsimd", g_s1, reads=["rt_eq"], writes=["rt_eq"], tag=f"rs1{t}")

            def v_m2a(v):
                return v.tensor_tensor(rt_m[:], rt_sfc[:], rt_eq[:], ALU.add)
            OP("vector", v_m2a, reads=["rt_eq", "rt_sfc"], writes=["rt_m"],
               tag=f"rm2a{t}")

            def v_m2b(v):
                return v.reduce_max(rt_g1[:], m3g, axis=AX.X)
            OP("vector", v_m2b, reads=["rt_m"], writes=["rt_g1"], tag=f"rm2b{t}")

            def g_gs1(g):
                return g.tensor_tensor(rt_gs[:], rt_gs[:], rt_g1[:], ALU.add)
            OP("gpsimd", g_gs1, reads=["rt_g1", "rt_gs"], writes=["rt_gs"], tag=f"rgs1{t}")

            def g_gs2(g):
                return g.tensor_copy(rt_gs0[:], rt_gs[:])
            OP("gpsimd", g_gs2, reads=["rt_gs"], writes=["rt_gs0"], tag=f"rgs2{t}")

            for it in range(4):
                def v_red(v):
                    return v.reduce_max(rt_r2[:], gs3, axis=AX.X)
                OP("vector", v_red, reads=["rt_gs"], writes=["rt_r2"], tag=f"rt4r{t}_{it}")

                def g_bnc(g):
                    return g.tensor_copy(rt_r1[:], rt_r2[:])
                OP("gpsimd", g_bnc, reads=["rt_r2"], writes=["rt_r1"], tag=f"rt4b{t}_{it}")
                if it < 3:
                    def v_eq(v):
                        return v.tensor_tensor(g13, gs3, r1bg, ALU.is_equal)
                    OP("vector", v_eq, reads=["rt_r1", "rt_gs"], writes=["rt_g1"],
                       tag=f"rt4e{t}_{it}")

                    def g_m1(g):
                        return g.tensor_scalar(rt_g1[:], rt_g1[:], NEGBIG, None, ALU.mult)
                    OP("gpsimd", g_m1, reads=["rt_g1"], writes=["rt_g1"],
                       tag=f"rt4m1{t}_{it}")

                    def g_m2(g):
                        return g.tensor_tensor(rt_gs[:], rt_gs[:], rt_g1[:], ALU.add)
                    OP("gpsimd", g_m2, reads=["rt_g1", "rt_gs"], writes=["rt_gs"],
                       tag=f"rt4m2{t}_{it}")

            def v_gmask(v):
                return v.tensor_tensor(g13, gs03, r1bg, ALU.is_ge)
            OP("vector", v_gmask, reads=["rt_r1", "rt_gs0"], writes=["rt_g1"],
               tag=f"rgm{t}")

            def g_bm(g):
                return g.tensor_copy(rt_gs[:], rt_g1[:])
            OP("gpsimd", g_bm, reads=["rt_g1"], writes=["rt_gs"], tag=f"rbm{t}")

            def v_emask(v):
                return v.tensor_scalar(eq3, gsbg, 0.5, None, ALU.is_le)
            OP("vector", v_emask, reads=["rt_gs"], writes=["rt_eq"], tag=f"rem{t}")

            def g_s2(g):
                return g.tensor_scalar(rt_eq[:], rt_eq[:], NEGBIG, None, ALU.mult)
            OP("gpsimd", g_s2, reads=["rt_eq"], writes=["rt_eq"], tag=f"rs2{t}")

            def v_mask8a(v):
                return v.tensor_tensor(rt_m[:], rt_sfc[:], rt_eq[:], ALU.add)
            OP("vector", v_mask8a, reads=["rt_eq", "rt_sfc"], writes=["rt_m"],
               tag=f"rm8a{t}")

            def v_mask8b(v):
                return v.tensor_copy(rt_sel[:], rt_m[:])
            OP("vector", v_mask8b, reads=["rt_m"], writes=["rt_sel"], tag=f"rm8b{t}")

            for it in range(8):
                def v_red(v):
                    return v.reduce_max(rt_r2[:], m3t, axis=AX.X)
                OP("vector", v_red, reads=["rt_m"], writes=["rt_r2"], tag=f"rt8r{t}_{it}")

                def g_bnc(g):
                    return g.tensor_copy(rt_r1[:], rt_r2[:])
                OP("gpsimd", g_bnc, reads=["rt_r2"], writes=["rt_r1"], tag=f"rt8b{t}_{it}")
                if it < 7:
                    def v_eq(v):
                        return v.tensor_tensor(eq3t, m3t, r1be, ALU.is_equal)
                    OP("vector", v_eq, reads=["rt_r1", "rt_m"], writes=["rt_eq"],
                       tag=f"rt8e{t}_{it}")

                    def g_m(g):
                        return g.tensor_scalar(rt_eq[:], rt_eq[:], NEGBIG, None, ALU.mult)
                    OP("gpsimd", g_m, reads=["rt_eq"], writes=["rt_eq"],
                       tag=f"rt8s{t}_{it}")

                    def v_m(v):
                        return v.tensor_tensor(rt_m[:], rt_m[:], rt_eq[:], ALU.add)
                    OP("vector", v_m, reads=["rt_eq", "rt_m"], writes=["rt_m"],
                       tag=f"rt8a{t}_{it}")

            def v_sel(v):
                return v.tensor_tensor(sel3, sel3, r1be, ALU.is_ge)
            OP("vector", v_sel, reads=["rt_r1", "rt_sel"], writes=["rt_sel"],
               tag=f"rsel{t}")

            def v_twa(v):
                return v.tensor_tensor(rt_m[:], rt_sc[:], rt_sel[:], ALU.mult)
            OP("vector", v_twa, reads=["rt_sel", "rt_sc"], writes=["rt_m"],
               tag=f"rtwa{t}")

            def v_twb(v):
                return v.reduce_sum(rt_r2[:], m3t, axis=AX.X)
            OP("vector", v_twb, reads=["rt_m"], writes=["rt_r2"], tag=f"rtwb{t}")

            def g_den(g):
                return g.tensor_scalar(rt_r1[:], rt_r2[:], 1e-20, None, ALU.add)
            OP("gpsimd", g_den, reads=["rt_r2"], writes=["rt_r1"], tag=f"rden{t}")

            def v_rec(v):
                return v.reciprocal(rt_r2[:], rt_r1[:])
            OP("vector", v_rec, reads=["rt_r1"], writes=["rt_r2"], tag=f"rrec{t}")

            def g_rdc(g):
                return g.tensor_copy(rt_r1[:], rt_r2[:])
            OP("gpsimd", g_rdc, reads=["rt_r2"], writes=["rt_r1"], tag=f"rrdc{t}")

            def v_comba(v):
                return v.scalar_tensor_tensor(m3t, m3t, MOE_SCALE, r1be, ALU.mult,
                                              ALU.mult)
            OP("vector", v_comba, reads=["rt_r1", "rt_m"], writes=["rt_m"],
               tag=f"rcba{t}")

            def v_combb(v):
                return v.tensor_copy(rt_mb[:], rt_m[:])
            OP("vector", v_combb, reads=["rt_m"], writes=["rt_mb"], tag=f"rcbb{t}")

        def up_dma(t, u):
            iu = u % 2

            def f(g, u=u, iu=iu):
                return g.dma_start(out=wslot[iu][:], in_=up_hbm[u])
            OP("gpsimd", f, writes=[("wslot", iu)], dma="dw%d" % iu, tag=f"wup{t}_{u}")

        def down_dma(t, u):
            idn = 2 + u % 2

            def f(g, u=u, idn=idn):
                return g.dma_start(out=wslot[idn][:], in_=down_hbm[u])
            OP("gpsimd", f, writes=[("wslot", idn)], dma="dw%d" % idn, tag=f"wdn{t}_{u}")

        def ffn_tile(t):
            unBf = unB[:].rearrange("p a b -> p (a b)")
            h_aps = [unA[:, 0:IBLK, :].rearrange("p a b -> p (a b)"),
                     unA[:, IBLK:2 * IBLK, :].rearrange("p a b -> p (a b)"),
                     unBf[:, 0:IBLK * CT], unBf[:, IBLK * CT:2 * IBLK * CT]]
            h_keys = ["unA", "unA", "unB", "unB"]

            for g0 in range(0, NUNIT, EG):
                units = list(range(g0, min(g0 + EG, NUNIT)))
                for j, u in enumerate(units):
                    if u + 1 < NUNIT:
                        up_dma(t, u + 1)
                    if j < 2:
                        down_dma(t, u)
                    routed = u < E
                    iu = u % 2
                    if routed:
                        cbank = psA[2 + u % 2]
                        ckey = ("psA", 2 + u % 2)

                        def mm_c(tn, u=u, cbank=cbank):
                            last = None
                            for tb in range(TB):
                                col = bass.AP(rt_mb.ap().tensor, tb * E + u,
                                              [[TB * E, 128], [0, 128]])
                                last = tn.matmul(cbank[:, tb * 128:(tb + 1) * 128],
                                                 col, ident_bf[:],
                                                 start=True, stop=True)
                            return last
                        OP("tensor", mm_c, reads=["rt_mb", "consts"], writes=[ckey],
                           tag=f"cmm{t}_{u}")

                        def ev_c(s, u=u, cbank=cbank):
                            return s.copy(c_sb[u % 2][:], cbank[:])
                        OP("scalar", ev_c, reads=[ckey], writes=[("c", u % 2)],
                           tag=f"cev{t}_{u}")
                    upv = wslot[iu][:].rearrange("p (k i) -> p k i", k=KC)
                    for ib in range(IBLK):
                        bank = [ps0, ps1][ib % 2]

                        def mm(tn, ib=ib, bank=bank, upv=upv):
                            last = None
                            for kc in range(KC):
                                wap = upv[:, kc, ib * 128:(ib + 1) * 128]
                                last = tn.matmul(bank[:], wap, xn_sb[:, kc, :],
                                                 start=(kc == 0), stop=(kc == KC - 1))
                            return last
                        OP("tensor", mm, reads=["xn", ("wslot", iu)],
                           writes=[("psb", ib % 2)], tag=f"up{t}_{u}_{ib}")

                        def s_relu(s, ib=ib, bank=bank):
                            return s.activation(wb1[:, ib, :], bank[:], AF.Relu)
                        OP("scalar", s_relu, reads=[("psb", ib % 2)], writes=["wb1"],
                           tag=f"re{t}_{u}_{ib}")

                    if debug and t == NPRE and u == 0:
                        tap("ffn_relu0", wb1[:, 0:IBLK, :], ["wb1"])
                        tap("ffn_c0", c_sb[0][:], [("c", 0)])
                    def g_sq(g):
                        a = wb1[:, 0:IBLK, :].rearrange("p a b -> p (a b)")
                        return g.tensor_tensor(sqb[:].rearrange("p a b -> p (a b)"),
                                               a, a, ALU.mult)
                    OP("gpsimd", g_sq, reads=["wb1"], writes=["sqb"], tag=f"sq{t}_{u}")
                    if debug and t == NPRE and u == 0:
                        tap("ffn_sq0", sqb[:], ["sqb"])

                    if routed:
                        cbb = bc_ap(c_sb[u % 2], 128, IBLK, CT)

                        def v_h(v, j=j, cbb=cbb):
                            return v.tensor_tensor(h_aps[j],
                                                   sqb[:].rearrange("p a b -> p (a b)"),
                                                   cbb, ALU.mult)
                        OP("vector", v_h, reads=["sqb", ("c", u % 2)],
                           writes=[h_keys[j]], tag=f"h{t}_{u}")
                    else:
                        def v_h(v, j=j):
                            return v.tensor_copy(h_aps[j],
                                                 sqb[:].rearrange("p a b -> p (a b)"))
                        OP("vector", v_h, reads=["sqb"], writes=[h_keys[j]],
                           tag=f"h{t}_{u}")
                for j, u in enumerate(units):
                    idn = 2 + u % 2

                    def mm(tn, j=j, u=u, idn=idn, first=(j == 0), last_u=(j == len(units) - 1)):
                        dnv = wslot[idn][:].rearrange("p (c h) -> p c h", c=ICH)
                        hap = h_aps[j]
                        last = None
                        for hb in range(HB):
                            for ic in range(ICH):
                                wap = dnv[:, ic, hb * 128:(hb + 1) * 128]
                                last = tn.matmul(dn_banks[hb], wap,
                                                 hap[:, ic * CT:(ic + 1) * CT],
                                                 start=(first and ic == 0),
                                                 stop=(last_u and ic == ICH - 1))
                        return last
                    OP("tensor", mm,
                       reads=[h_keys[j], ("wslot", idn)],
                       writes=list(set(dn_keys)), tag=f"dn{t}_{u}")
                    if j + 2 < len(units):
                        down_dma(t, units[j + 2])
                for hb in range(HB):
                    def v_fl(v, hb=hb):
                        return v.tensor_tensor(acc_sb[:, hb, :], acc_sb[:, hb, :],
                                               dn_banks[hb], ALU.add)
                    OP("vector", v_fl, reads=[dn_keys[hb], ("acc", hb)],
                       writes=[("acc", hb)], tag=f"fl{t}_{g0}_{hb}")
                if debug and t == NPRE and g0 == 0:
                    tap("ffn_h0", unA[:], ["unA"])
                    tap("ffn_acc0", acc_sb[:], [("acc", hb2) for hb2 in range(PB)], dt=F32)

        def out_tile(t):
            def f(sy, t=t):
                return sy.dma_start(out=out_hbm[t - NPRE], in_=acc_sb[:])
            OP("sync", f, reads=[("acc", dvb) for dvb in range(PB)], writes=["outdma"],
               dma="dout", tag=f"out{t}")

        def dma_w2(t, src, lo, tag):
            def f0(g):
                return g.dma_start(out=wslot[lo][:], in_=src[:, 0:KC * IM])
            OP("gpsimd", f0, writes=[("wslot", lo)], dma="dw%d" % lo, tag=f"{tag}a{t}")

            def f1(g):
                return g.dma_start(out=wslot[lo + 1][:], in_=src[:, KC * IM:KC * H])
            OP("gpsimd", f1, writes=[("wslot", lo + 1)], dma="dw%d" % (lo + 1),
               tag=f"{tag}b{t}")

        for t in range(NTILE):
            own = t >= NPRE
            dma_w2(t, wk_hbm, 2, "wk")
            load_x(t)
            pos_tile(t)
            ln_tile(t)
            sin_tables(t)
            proj_fm(t, 2, unA, "unA", "k")
            xpos_apply(t, -1.0, unA, "unA")
            proj_tm_v(t)
            if t == NPRE - 1:
                tap("ktm11", unB[:], ["unB"])
                tap("vtm11", vtm_sb[:], ["vtm"])
            if own:
                if t == NPRE:
                    tap("gcol", gcol[:], ["gcol"], dt=F32)
                    tap("invf", invf_row[:], ["featrows"], dt=F32)
                    tap("lnsc", lnsc_row[:], ["featrows"], dt=F32)
                    tap("tabC", tabC[:], ["tabC"])
                    tap("tabS", tabS[:], ["tabS"])
                    tap("alt", altcol[:], ["alt"], dt=F32)
                    tap("xn", xn_sb[:], ["xn"])
                    tap("kx", unA[:], ["unA"])
                    tap("vtm", vtm_sb[:], ["vtm"])
                    tap("s_in", s_sb[:], ["S"])
                dma_w2(t, wq_hbm, 0, "wq")
                proj_fm(t, 0, qx_sb, "qx", "q")
                xpos_apply(t, 1.0, qx_sb, "qx")
                if t == NPRE:
                    tap("qx", qx_sb[:], ["qx"])
                k_transpose(t)
                attn_tile(t)
                if t == NPRE:
                    tap("sct", sqb[:], ["sqb"])
                    tap("acc_attn", acc_sb[:], [("acc", dvb) for dvb in range(PB)], dt=F32)
                state_update(t)
                routing_tile(t)
                if t == NPRE:
                    tap("comb", rt_m[:], ["rt_m", "rt_mb"], dt=F32)
                    tap("rtsc", rt_sc[:], ["rt_sc"], dt=F32)
                    tap("rtsfc", rt_sfc[:], ["rt_sfc"], dt=F32)
                    tap("rtgs", rt_gs[:], ["rt_gs"], dt=F32)
                    tap("rtg1", rt_g1[:], ["rt_gmask"], dt=F32)
                    tap("rtsel", rt_sel[:], ["rt_sel"], dt=F32)
                up_dma(t, 0)
                ffn_tile(t)
                if t == NPRE:
                    tap("acc_ffn", acc_sb[:], [("acc", dvb) for dvb in range(PB)], dt=F32)
                out_tile(t)
            else:
                k_transpose(t)
                state_update(t)

        def f_fin(sy):
            return sy.nop()
        OP("sync", f_fin, reads=["outdma"], tag="final_wait")

        with nc.Block() as block:
            sch.build(block, sems, waitall=("dst",))

    return nc


def _pack_fm(w):
    R, Cc = w.shape
    return np.ascontiguousarray(
        w.reshape(R // 128, 128, Cc).transpose(1, 0, 2).reshape(128, (R // 128) * Cc))


def kernel(L0_attn_X, L0_attn_W_Q, L0_attn_W_K, L0_attn_W_V, L0_ffn_router_w,
           L0_ffn_correction_bias, L0_ffn_up_w, L0_ffn_down_w,
           L0_ffn_shared_up_w, L0_ffn_shared_down_w):
    X = np.asarray(L0_attn_X, dtype=np.float32)
    wq = _pack_fm(np.asarray(L0_attn_W_Q, dtype=np.float32))
    wk = _pack_fm(np.asarray(L0_attn_W_K, dtype=np.float32))
    wv = _pack_fm(np.asarray(L0_attn_W_V, dtype=np.float32))
    router = _pack_fm(np.ascontiguousarray(
        np.asarray(L0_ffn_router_w, dtype=np.float32).T))
    bias = np.ascontiguousarray(
        np.asarray(L0_ffn_correction_bias, dtype=np.float32).reshape(1, E))
    up_w = np.asarray(L0_ffn_up_w, dtype=np.float32)
    down_w = np.asarray(L0_ffn_down_w, dtype=np.float32)
    sup = np.asarray(L0_ffn_shared_up_w, dtype=np.float32)
    sdown = np.asarray(L0_ffn_shared_down_w, dtype=np.float32)

    up_all = np.empty((NUNIT, 128, KC * IM), dtype=np.float32)
    down_all = np.empty((NUNIT, 128, ICH * H), dtype=np.float32)
    for e in range(E):
        up_all[e] = _pack_fm(up_w[e])
        down_all[e] = _pack_fm(down_w[e])
    for s in range(NSH):
        up_all[E + s] = _pack_fm(sup[:, s * IM:(s + 1) * IM])
        down_all[E + s] = _pack_fm(sdown[s * IM:(s + 1) * IM, :])

    nc = build_program()
    pair_row = (np.arange(H, dtype=np.int64) // 2).astype(np.float32).reshape(1, H)
    alt_col = np.where(np.arange(128) % 2 == 0, -1.0, 1.0).astype(np.float32).reshape(128, 1)

    in_maps = []
    for c in range(NCORE):
        b = c // CORES_PER_SEQ
        s0 = (c % CORES_PER_SEQ) * OWN
        xt = np.zeros((NTILE, 128, PB * CT), dtype=np.float32)
        lo = s0 - PRE
        for t in range(NTILE):
            a = lo + t * CT
            if a + CT <= 0:
                continue
            seg = X[b, max(a, 0):a + CT, :]
            n = seg.shape[0]
            fm = seg.T.reshape(PB, 128, n).transpose(1, 0, 2)
            buf = np.zeros((128, PB, CT), dtype=np.float32)
            buf[:, :, CT - n:] = fm
            xt[t] = buf.reshape(128, PB * CT)
        base = np.array([[float(s0 - PRE)]], dtype=np.float32)
        in_maps.append({
            "x_hbm": xt, "wq_hbm": wq, "wk_hbm": wk, "wv_hbm": wv,
            "router_hbm": router, "bias_hbm": bias, "up_hbm": up_all,
            "down_hbm": down_all, "base_hbm": base,
            "pair_hbm": pair_row, "altc_hbm": alt_col,
        })

    res = run_bass_kernel_spmd(nc, in_maps, core_ids=list(range(NCORE)))

    out = np.empty((B, L, H), dtype=np.float32)
    for c in range(NCORE):
        b = c // CORES_PER_SEQ
        s0 = (c % CORES_PER_SEQ) * OWN
        o = np.asarray(res.results[c]["out_hbm"]).reshape(NOWN, 128, PB, CT)
        tm = o.transpose(0, 3, 2, 1).reshape(NOWN * CT, H)
        out[b, s0:s0 + OWN, :] = tm
    return out


# version 09
# speedup vs baseline: 1.0306x; 1.0306x over previous
from contextlib import ExitStack
import math
import numpy as np
import concourse.bass as bass
import concourse.mybir as mybir
from concourse.bass_utils import run_bass_kernel_spmd

F32 = mybir.dt.float32
BF16 = mybir.dt.bfloat16
I32 = mybir.dt.int32
AF = mybir.ActivationFunctionType
ALU = mybir.AluOpType
AX = mybir.AxisListType

B, L, H = 2, 8192, 1024
NCORE = 8
CORES_PER_SEQ = 4
OWN = L // CORES_PER_SEQ
CT = 512
NOWN = OWN // CT
PRE = OWN * (CORES_PER_SEQ - 1)
NPRE = PRE // CT
NTILE = NPRE + NOWN
E = 64
IM = 512
ISH = 2048
NSH = ISH // IM
NUNIT = E + NSH
EG = 4
GAMMA = 31.0 / 32.0
LNG = math.log(GAMMA)
SCALE_BASE = 512.0
MOE_SCALE = 2.5
EPS = 1e-5
TWO_PI = 2.0 * math.pi
INV_2PI = 1.0 / (2.0 * math.pi)
PI_HALF = math.pi / 2.0
CLS = 0.49999994
NEGBIG = -1.0e30
PB = H // 128
KC = PB
IBLK = IM // 128
ICH = IM // 128
HB = PB
TB = CT // 128

ENGINES = ("tensor", "vector", "scalar", "gpsimd", "sync")


class _Op:
    __slots__ = ("engine", "fn", "deps", "need_inc", "inc_val", "dma", "tag")

    def __init__(self, engine, fn, dma, tag):
        self.engine, self.fn, self.dma, self.tag = engine, fn, dma, tag
        self.deps = []
        self.need_inc = False
        self.inc_val = None


class Sched:
    def __init__(self):
        self.per_engine = {e: [] for e in ENGINES}
        self.res = {}
        self.ops = []

    def op(self, engine, fn, reads=(), writes=(), dma=None, tag=""):
        o = _Op(engine, fn, dma, tag)
        deps = []
        for r in reads:
            st = self.res.get(r)
            if st and st["w"] is not None:
                deps.append(st["w"])
        for w in writes:
            st = self.res.get(w)
            if st:
                if st["w"] is not None:
                    deps.append(st["w"])
                deps.extend(st["r"])
        seen = set()
        for d in deps:
            if id(d) in seen:
                continue
            seen.add(id(d))
            if (not d.dma) and d.engine == engine == "tensor":
                continue
            o.deps.append(d)
        for r in reads:
            self.res.setdefault(r, {"w": None, "r": []})["r"].append(o)
        for w in writes:
            self.res[w] = {"w": o, "r": []}
        self.ops.append(o)
        self.per_engine[engine].append(o)
        return o

    def build(self, block, sems, waitall=()):
        for o in self.ops:
            for d in o.deps:
                d.need_inc = True
        cum = {}

        def semkey(o):
            return o.dma if o.dma else o.engine

        for e in ENGINES:
            for o in self.per_engine[e]:
                if o.dma:
                    o.need_inc = True
                if o.need_inc:
                    k = semkey(o)
                    cum[k] = cum.get(k, 0) + (16 if o.dma else 1)
                    o.inc_val = cum[k]
        for e in ENGINES:
            for o in self.per_engine[e]:
                if o.dma in waitall:
                    o.inc_val = cum[o.dma]

        def emit_for(engine, eng):
            waited = {}
            for o in self.per_engine[engine]:
                for d in o.deps:
                    k = semkey(d)
                    v = d.inc_val
                    if waited.get(k, 0) >= v:
                        continue
                    eng.wait_ge(sems[k], v)
                    waited[k] = v
                ins = o.fn(eng)
                if o.need_inc:
                    assert ins is not None, f"op {o.tag} returned no instruction"
                    ins.then_inc(sems[semkey(o)], 16 if o.dma else 1)

        block.tensor(lambda eng: emit_for("tensor", eng))
        block.vector(lambda eng: emit_for("vector", eng))
        block.scalar(lambda eng: emit_for("scalar", eng))
        block.gpsimd(lambda eng: emit_for("gpsimd", eng))
        block.sync(lambda eng: emit_for("sync", eng))
        return cum


def bc_ap(sbt, n_p, rep, inner, off=0):
    return bass.AP(sbt.ap().tensor, off, [[inner, n_p], [0, rep], [1, inner]])


def build_program(debug=False):
    nc = bass.Bass("TRN2", target_bir_lowering=False)
    dr = lambda name, shape, dt, kind: nc.dram_tensor(name, shape, dt, kind=kind).ap()
    x_hbm = dr("x_hbm", [NTILE, 128, PB * CT], F32, "ExternalInput")
    wq_hbm = dr("wq_hbm", [128, KC * H], F32, "ExternalInput")
    wk_hbm = dr("wk_hbm", [128, KC * H], F32, "ExternalInput")
    wv_hbm = dr("wv_hbm", [128, KC * H], F32, "ExternalInput")
    router_hbm = dr("router_hbm", [128, KC * E], F32, "ExternalInput")
    bias_hbm = dr("bias_hbm", [1, E], F32, "ExternalInput")
    up_hbm = dr("up_hbm", [NUNIT, 128, KC * IM], F32, "ExternalInput")
    down_hbm = dr("down_hbm", [NUNIT, 128, ICH * H], F32, "ExternalInput")
    base_hbm = dr("base_hbm", [1, 1], F32, "ExternalInput")
    pair_hbm = dr("pair_hbm", [1, H], F32, "ExternalInput")
    altc_hbm = dr("altc_hbm", [128, 1], F32, "ExternalInput")
    out_hbm = dr("out_hbm", [NOWN, 128, PB * CT], F32, "ExternalOutput")

    sch = Sched()
    es = ExitStack()
    with es:
        def sb(n, sh, dt):
            return es.enter_context(nc.sbuf_tensor(n, sh, dt))

        def psb(n, sh):
            return es.enter_context(nc.psum_tensor(n, sh, F32))

        wv_sb = sb("wv_sb", [128, KC, H], BF16)
        router_sb = sb("router_sb", [128, KC, E], BF16)
        d_sb = sb("d_sb", [128, TB, CT], BF16)
        ident_bf = sb("ident_bf", [128, 128], BF16)
        gcol = sb("gcol", [128, TB], F32)
        growbc = sb("growbc", [128, CT], BF16)
        altcol = sb("altcol", [128, 1], F32)
        bias_bc = sb("bias_bc", [128, E], F32)
        negbig_col = sb("negbig_col", [128, 1], F32)
        b2048 = sb("b2048", [128, 1], F32)
        iota_row = sb("iota_row", [1, CT], F32)
        invf_row = sb("invf_row", [1, H], F32)
        lnsc_row = sb("lnsc_row", [1, H], F32)
        rows = sb("rows", [1, 4 * CT], F32)
        hrow = rows[0:1, 0:H]
        hrow2 = rows[0:1, H:2 * H]
        base_sb = sb("base_sb", [1, 1], F32)
        pos_row = sb("pos_row", [1, CT], F32)
        bias_row = sb("bias_row", [1, E], F32)
        growrow = sb("growrow", [1, CT], F32)
        onesf_row = sb("onesf_row", [1, 128], F32)
        _x1 = sb("x_sb", [128, PB, CT], BF16)
        x_sb = [_x1, _x1]
        xn_sb = sb("xn_sb", [128, PB, CT], BF16)
        wb1 = sb("wb1", [128, PB, CT], BF16)
        sqb = sb("sqb", [128, IBLK, CT], BF16)
        sct_sb = sqb
        qx_sb = sb("qx_sb", [128, PB, CT], BF16)
        vtm_sb = sb("vtm_sb", [128, TB, H], BF16)
        tabC = sb("tabC", [128, PB, CT], BF16)
        tabS = sb("tabS", [128, PB, CT], BF16)
        s_sb = sb("s_sb", [128, PB, H], BF16)
        acc_sb = sb("acc_sb", [128, PB, CT], F32)
        c_sb = [sb(f"c_sb{i}", [128, CT], BF16) for i in range(2)]
        unA = sb("unA", [128, PB, CT], BF16)
        unB = sb("unB", [128, TB, H], BF16)
        wslot = [sb(f"wslot{i}", [128, KC * IM], BF16) for i in range(4)]
        tg_u = sb("tg_u", [128, CT], F32)
        tg_ni = sb("tg_ni", [128, CT], I32)
        tg_nf = sb("tg_nf", [128, CT], F32)
        tg_ds = sb("tg_ds", [128, CT], F32)
        tg_dc = sb("tg_dc", [128, CT], F32)
        tg_sc = sb("tg_sc", [128, CT], F32)
        rt_sc = sb("rt_sc", [128, TB * E], F32)
        rt_sfc = sb("rt_sfc", [128, TB * E], F32)
        rt_m = sb("rt_m", [128, TB * E], F32)
        rt_sel = sb("rt_sel", [128, TB * E], F32)
        rt_eq = sb("rt_eq", [128, TB * E], F32)
        rt_mb = sb("rt_mb", [128, TB * E], BF16)
        rt_gs = sb("rt_gs", [128, TB * PB], F32)
        rt_gs0 = sb("rt_gs0", [128, TB * PB], F32)
        rt_g1 = sb("rt_g1", [128, TB * PB], F32)
        rt_r1 = sb("rt_r1", [128, TB], F32)
        rt_r2 = sb("rt_r2", [128, TB], F32)

        ps0 = psb("ps0", [128, 512])
        ps1 = psb("ps1", [128, 512])
        psS = psb("psS", [128, 1024])
        psA = [psb(f"psA{i}", [128, 512]) for i in range(4)]
        dn_banks = [ps0[:], ps1[:], psS[:, 0:512], psS[:, 512:1024],
                    psA[0][:], psA[1][:], psA[2][:], psA[3][:]]
        dn_keys = [("psb", 0), ("psb", 1), "psS", "psS",
                   ("psA", 0), ("psA", 1), ("psA", 2), ("psA", 3)]

        sems = {}
        for e in ENGINES:
            sems[e] = es.enter_context(nc.semaphore("sem_" + e))
        for k in ("dst", "dx", "dw0", "dw1", "dw2", "dw3", "dout", "dtap"):
            sems[k] = es.enter_context(nc.semaphore("sem_" + k))

        OP = sch.op
        dbg_n = [0]

        def tap(name, ap, reads, dt=BF16):
            if not debug:
                return
            shp = list(ap.shape)
            d = nc.dram_tensor(f"dbg_{name}", shp, dt, kind="ExternalOutput").ap()

            def f(sy, ap=ap, d=d):
                return sy.dma_start(out=d, in_=ap)
            OP("sync", f, reads=list(reads), writes=[f"dbgw_{name}", "tapchain"],
               dma="dtap", tag=f"tap_{name}")
            dbg_n[0] += 1

        ones_bf_ap = nc.const_aps.tensor(1.0, [128, 1], BF16)
        nc.const_aps.register(F32, 2048.0, b2048.ap())

        def g_setup(g):
            g.memset(onesf_row[:], 1.0)
            g.memset(negbig_col[:], NEGBIG)
            g.memset(b2048[:], 2048.0)
            g.iota(iota_row[:], pattern=[[1, CT]], channel_multiplier=0,
                   allow_small_or_imprecise_dtypes=True)
            g.iota(tg_ds[:, 0:128], pattern=[[-1, 128]], channel_multiplier=1,
                   allow_small_or_imprecise_dtypes=True)
            return g.iota(gcol[:], pattern=[[128, TB]], channel_multiplier=1,
                          allow_small_or_imprecise_dtypes=True)
        OP("gpsimd", g_setup, writes=["consts", "gcol_i",
                                      "tg_u", "tg_ni", "tg_nf", "tg_ds", "rows"], tag="setup_g")

        def g_id1(g):
            return g.tensor_scalar(tg_ds[:, 0:128], tg_ds[:, 0:128], 0.0, None,
                                   ALU.is_equal)
        OP("gpsimd", g_id1, reads=["tg_ds"], writes=["tg_ds"], tag="setup_id1")

        def g_id2(g):
            return g.tensor_copy(ident_bf[:], tg_ds[:, 0:128])
        OP("gpsimd", g_id2, reads=["tg_ds"], writes=["consts"], tag="setup_id2")

        def mk_dma(dst, src, wkey, tag):
            def f(g):
                return g.dma_start(out=dst, in_=src)
            OP("gpsimd", f, writes=[wkey], dma="dst", tag=tag)

        mk_dma(hrow2, pair_hbm, "featrows_i", "dma_pair")
        mk_dma(altcol[:], altc_hbm, "alt", "dma_alt")

        def g_gcs(g):
            return g.tensor_scalar(gcol[:], gcol[:], -float(CT), None, ALU.add)
        OP("gpsimd", g_gcs, reads=["gcol_i"], writes=["gcol_i"], tag="setup_gcs")

        def s_gcol(s):
            return s.activation(gcol[:], gcol[:], AF.Exp, scale=-LNG)
        OP("scalar", s_gcol, reads=["gcol_i"], writes=["gcol"], tag="setup_gcol")



        def v_grow(v):
            return v.tensor_scalar(growrow[:], iota_row[:], LNG, None, ALU.mult)
        OP("vector", v_grow, reads=["consts"], writes=["growr"], tag="setup_growr1")

        def s_grow(s):
            return s.activation(growrow[:], growrow[:], AF.Exp)
        OP("scalar", s_grow, reads=["growr"], writes=["growr2"], tag="setup_growr2")

        def mm_growbc(tn):
            return tn.matmul(ps0[:, 0:CT], onesf_row[:], growrow[:],
                             start=True, stop=True)
        OP("tensor", mm_growbc, reads=["growr2", "consts"], writes=[("psb", 0)],
           tag="setup_growbc1")

        def v_growbc(v):
            return v.tensor_copy(growbc[:], ps0[:, 0:CT])
        OP("vector", v_growbc, reads=[("psb", 0)], writes=["growbc"], tag="setup_growbc2")

        def v_featv(v):
            v.tensor_scalar(invf_row[:], hrow2, -math.log(10000.0) / (H // 2),
                            None, ALU.mult)
            return v.tensor_scalar(lnsc_row[:], hrow2, 2.0, 0.4 * H, ALU.mult,
                                   ALU.add)
        OP("vector", v_featv, reads=["featrows_i", "rows", "consts"], writes=["featrows_v"],
           tag="setup_featv")

        def v_featv2(v):
            return v.tensor_scalar(lnsc_row[:], lnsc_row[:], 1.0 / (1.4 * H), None,
                                   ALU.mult)
        OP("vector", v_featv2, reads=["featrows_v"], writes=["featrows_v"],
           tag="setup_featv2")

        def s_feat(s):
            s.activation(invf_row[:], invf_row[:], AF.Exp)
            return s.activation(lnsc_row[:], lnsc_row[:], AF.Ln)
        OP("scalar", s_feat, reads=["featrows_v"], writes=["featrows_s"], tag="setup_feats")

        def v_featf(v):
            return v.tensor_scalar(lnsc_row[:], lnsc_row[:], 1.0 / SCALE_BASE, None,
                                   ALU.mult)
        OP("vector", v_featf, reads=["featrows_s"], writes=["featrows"], tag="setup_featf")

        mk_dma(wv_sb[:], wv_hbm, "wv", "dma_wv")
        mk_dma(router_sb[:], router_hbm, "router", "dma_router")
        mk_dma(bias_row[:], bias_hbm, "bias_r", "dma_bias")
        mk_dma(base_sb[:], base_hbm, "base", "dma_base")

        def mm_biasbc(tn):
            return tn.matmul(ps1[:, 0:E], onesf_row[:], bias_row[:],
                             start=True, stop=True)
        OP("tensor", mm_biasbc, reads=["bias_r", "consts"], writes=[("psb", 1)],
           tag="setup_biasbc1")

        def s_biasbc(s):
            return s.copy(bias_bc[:], ps1[:, 0:E])
        OP("scalar", s_biasbc, reads=[("psb", 1)], writes=["bias_bc"], tag="setup_biasbc2")

        for kb in range(TB):
            def g_d(g, kb=kb):
                return g.iota(tg_u[:], pattern=[[1, CT]], channel_multiplier=-1,
                              base=-kb * 128, allow_small_or_imprecise_dtypes=True)
            OP("gpsimd", g_d, writes=["tg_u"], tag=f"dgen{kb}")

            def s_d(s):
                return s.activation(tg_ds[:], tg_u[:], AF.Exp, scale=LNG)
            OP("scalar", s_d, reads=["tg_u"], writes=["tg_ds"], tag=f"dexp{kb}")

            def v_d1(v):
                return v.tensor_scalar(tg_dc[:], tg_u[:], 0.0, None, ALU.is_ge)
            OP("vector", v_d1, reads=["tg_u"], writes=["tg_dc"], tag=f"dge{kb}")

            def v_d2(v, kb=kb):
                return v.tensor_tensor(d_sb[:, kb, :], tg_ds[:], tg_dc[:], ALU.mult)
            OP("vector", v_d2, reads=["tg_ds", "tg_dc"], writes=[("dmask", kb)],
               tag=f"dmask{kb}")

        def v_zs(v):
            return v.memset(s_sb[:], 0.0)
        OP("vector", v_zs, writes=["S"], tag="zeroS")

        def load_x(t):
            def f(g, t=t):
                return g.dma_start(out=x_sb[t % 2][:], in_=x_hbm[t])
            OP("gpsimd", f, writes=["x"], dma="dx", tag=f"ldx{t}")

        def pos_tile(t):
            def f(v, t=t):
                return v.tensor_scalar(pos_row[:], iota_row[:], base_sb[0:1, 0:1],
                                       float(t * CT), ALU.add, ALU.add)
            OP("vector", f, reads=["base", "consts"], writes=["pos_row"], tag=f"pos{t}")

        def ln_tile(t):
            xb = x_sb[t % 2]

            def mm_sum(tn, xb=xb):
                last = None
                for pb_ in range(PB):
                    last = tn.matmul(ps0[0:1, 0:CT], ones_bf_ap, xb[:, pb_, :],
                                     start=(pb_ == 0), stop=(pb_ == PB - 1))
                return last
            OP("tensor", mm_sum, reads=["x"], writes=[("psb", 0)], tag=f"lnsum{t}")

            def s_sq(s, xb=xb):
                return s.activation(wb1[:], xb[:], AF.Square)
            OP("scalar", s_sq, reads=["x"], writes=["wb1"], tag=f"lnsq{t}")

            def mm_sq(tn):
                last = None
                for pb_ in range(PB):
                    last = tn.matmul(ps1[0:1, 0:CT], ones_bf_ap, wb1[:, pb_, :],
                                     start=(pb_ == 0), stop=(pb_ == PB - 1))
                return last
            OP("tensor", mm_sq, reads=["wb1"], writes=[("psb", 1)], tag=f"lnsum2_{t}")

            MEAN, WORK, RSTD, MR = 0, 1, 2, 3
            rr = lambda i: rows[0:1, i * CT:(i + 1) * CT]

            def v_rows(v):
                v.tensor_scalar(rr(MEAN), ps0[0:1, 0:CT], 1.0 / H, None, ALU.mult)
                return v.tensor_scalar(rr(WORK), ps1[0:1, 0:CT], 1.0 / H, None,
                                       ALU.mult)
            OP("vector", v_rows, reads=[("psb", 0), ("psb", 1)], writes=["rows"],
               tag=f"lnrows{t}")

            def v_rows2(v):
                return v.tensor_tensor(rr(RSTD), rr(MEAN), rr(MEAN), ALU.mult)
            OP("vector", v_rows2, reads=["rows"], writes=["rows"], tag=f"lnrsq{t}")

            def v_rows3(v):
                return v.scalar_tensor_tensor(rr(WORK), rr(WORK), EPS, rr(RSTD),
                                              ALU.add, ALU.subtract)
            OP("vector", v_rows3, reads=["rows"], writes=["rows"], tag=f"lnvar{t}")

            def s_rstd(s):
                return s.activation(rr(WORK), rr(WORK), AF.Ln)
            OP("scalar", s_rstd, reads=["rows"], writes=["rows"], tag=f"lnrstd{t}")

            def s_rstd2(s):
                return s.activation(rr(RSTD), rr(WORK), AF.Exp, scale=-0.5)
            OP("scalar", s_rstd2, reads=["rows"], writes=["rows"], tag=f"lnrstd2{t}")

            def v_mr(v):
                return v.tensor_tensor(rr(MR), rr(MEAN), rr(RSTD), ALU.mult)
            OP("vector", v_mr, reads=["rows"], writes=["rows"], tag=f"lnmr{t}")

            def mm_bc(tn):
                tn.matmul(ps0[:, 0:CT], onesf_row[:], rr(RSTD), start=True, stop=True)
                return tn.matmul(ps1[:, 0:CT], onesf_row[:], rr(MR), start=True,
                                 stop=True)
            OP("tensor", mm_bc, reads=["rows", "consts"],
               writes=[("psb", 0), ("psb", 1)], tag=f"lnbc{t}")

            rb = bass.AP(ps0.ap().tensor, 0, [[512, 128], [0, PB], [1, CT]])
            mb = bass.AP(ps1.ap().tensor, 0, [[512, 128], [0, PB], [1, CT]])

            def v_xn(v, xb=xb, rb=rb):
                return v.tensor_tensor(xn_sb[:], xb[:], rb, ALU.mult)
            OP("vector", v_xn, reads=["x", ("psb", 0)], writes=["xn"], tag=f"xn{t}")

            def v_xn2(v, mb=mb):
                return v.tensor_tensor(xn_sb[:], xn_sb[:], mb, ALU.subtract)
            OP("vector", v_xn2, reads=["xn", ("psb", 1)], writes=["xn"], tag=f"xn2{t}")

        def sin_tables(t):
            for pbi in range(PB):
                def mm_th(tn, pbi=pbi):
                    return tn.matmul([ps0, ps1][pbi % 2][:, 0:CT],
                                     invf_row[0:1, pbi * 128:(pbi + 1) * 128],
                                     pos_row[:], start=True, stop=True)
                OP("tensor", mm_th, reads=["pos_row", "featrows"],
                   writes=[("psb", pbi % 2)], tag=f"th{t}_{pbi}")

                def s_u(s, pbi=pbi):
                    return s.activation(tg_u[:], [ps0, ps1][pbi % 2][:, 0:CT], AF.Copy,
                                        scale=INV_2PI, bias=2048.0)
                OP("scalar", s_u, reads=[("psb", pbi % 2), "consts"], writes=["tg_u"],
                   tag=f"tu{t}_{pbi}")

                def g_cast1(g):
                    return g.tensor_copy(tg_ni[:], tg_u[:])
                OP("gpsimd", g_cast1, reads=["tg_u"], writes=["tg_ni"],
                   tag=f"tc1{t}_{pbi}")

                def g_cast2(g):
                    return g.tensor_copy(tg_nf[:], tg_ni[:])
                OP("gpsimd", g_cast2, reads=["tg_ni"], writes=["tg_nf"],
                   tag=f"tc2{t}_{pbi}")

                fold_steps = [
                    (lambda v: v.tensor_tensor(tg_ds[:], tg_u[:], tg_nf[:],
                                               ALU.subtract),
                     ["tg_nf", "tg_u"], ["tg_ds"]),
                    (lambda v: v.tensor_scalar(tg_dc[:], tg_ds[:], 0.5, None,
                                               ALU.is_ge),
                     ["tg_ds"], ["tg_dc"]),
                    (lambda v: v.scalar_tensor_tensor(tg_ds[:], tg_dc[:], -1.0,
                                                      tg_ds[:], ALU.mult, ALU.add),
                     ["tg_dc", "tg_ds"], ["tg_ds"]),
                    (lambda v: v.tensor_scalar(tg_ds[:], tg_ds[:], -CLS, CLS,
                                               ALU.max, ALU.min),
                     ["tg_ds"], ["tg_ds"]),
                    (lambda v: v.tensor_scalar(tg_dc[:], tg_ds[:], 0.25, None,
                                               ALU.is_gt),
                     ["tg_ds"], ["tg_dc"]),
                    (lambda v: v.scalar_tensor_tensor(tg_dc[:], tg_dc[:], -1.0,
                                                      tg_ds[:], ALU.mult, ALU.add),
                     ["tg_dc", "tg_ds"], ["tg_dc"]),
                    (lambda v: v.tensor_scalar(tg_dc[:], tg_dc[:], 0.25, -CLS,
                                               ALU.add, ALU.max),
                     ["tg_dc"], ["tg_dc"]),
                    (lambda v: v.tensor_scalar(tg_dc[:], tg_dc[:], CLS, None,
                                               ALU.min),
                     ["tg_dc"], ["tg_dc"]),
                ]
                for fi, (ffn_, frd, fwr) in enumerate(fold_steps):
                    OP("vector", ffn_, reads=frd, writes=fwr, tag=f"tf{t}_{pbi}_{fi}")

                def s_sincos(s, pbi=pbi):
                    s.activation(tabS[:, pbi, :], tg_ds[:], AF.Sin, scale=TWO_PI)
                    return s.activation(tabC[:, pbi, :], tg_dc[:], AF.Sin,
                                        scale=TWO_PI)
                OP("scalar", s_sincos, reads=["tg_ds", "tg_dc", "consts"],
                   writes=["tabS", "tabC"], tag=f"tsin{t}_{pbi}")

                def v_alt(v, pbi=pbi):
                    return v.tensor_scalar(tabS[:, pbi, :], tabS[:, pbi, :], altcol[:],
                                           None, ALU.mult)
                OP("vector", v_alt, reads=["alt", "tabS"], writes=["tabS"],
                   tag=f"talt{t}_{pbi}")

        def xpos_apply(t, sgn, buf_sb, buf_key):
            mask = [i ^ 1 for i in range(32)]

            def v_sh(v, buf_sb=buf_sb):
                return v.stream_shuffle(wb1[:].rearrange("p a b -> p (a b)"),
                                        buf_sb[:].rearrange("p a b -> p (a b)"), mask)
            OP("vector", v_sh, reads=[buf_key], writes=["wb1"], tag=f"xsh{t}{buf_key}")

            for pbi in range(PB):
                def mm_ths(tn, pbi=pbi):
                    return tn.matmul([ps0, ps1][pbi % 2][:, 0:CT],
                                     lnsc_row[0:1, pbi * 128:(pbi + 1) * 128],
                                     pos_row[:], start=True, stop=True)
                OP("tensor", mm_ths, reads=["pos_row", "featrows"],
                   writes=[("psb", pbi % 2)], tag=f"ths{t}{buf_key}{pbi}")

                def s_sc(s, sgn=sgn, pbi=pbi):
                    return s.activation(tg_sc[:], [ps0, ps1][pbi % 2][:, 0:CT], AF.Exp,
                                        scale=sgn)
                OP("scalar", s_sc, reads=[("psb", pbi % 2)], writes=["tg_sc"],
                   tag=f"xsc{t}{buf_key}{pbi}")

                def g_t2(g, pbi=pbi):
                    return g.tensor_tensor(sqb[:, 0, :], wb1[:, pbi, :], tabS[:, pbi, :],
                                           ALU.mult)
                OP("gpsimd", g_t2, reads=["wb1", "tabS"], writes=["sqb"],
                   tag=f"xt2{t}{buf_key}{pbi}")

                def v_fin1(v, pbi=pbi, buf_sb=buf_sb):
                    return v.tensor_tensor(buf_sb[:, pbi, :], buf_sb[:, pbi, :],
                                           tabC[:, pbi, :], ALU.mult)
                OP("vector", v_fin1, reads=["tabC", buf_key], writes=[buf_key],
                   tag=f"xf1{t}{buf_key}{pbi}")

                def v_fin2(v, pbi=pbi, buf_sb=buf_sb):
                    return v.tensor_tensor(buf_sb[:, pbi, :], buf_sb[:, pbi, :],
                                           sqb[:, 0, :], ALU.add)
                OP("vector", v_fin2, reads=["sqb", buf_key], writes=[buf_key, "sqb"],
                   tag=f"xf2{t}{buf_key}{pbi}")

                def v_fin3(v, pbi=pbi, buf_sb=buf_sb):
                    return v.tensor_tensor(buf_sb[:, pbi, :], buf_sb[:, pbi, :],
                                           tg_sc[:], ALU.mult)
                OP("vector", v_fin3, reads=["tg_sc", buf_key], writes=[buf_key],
                   tag=f"xf3{t}{buf_key}{pbi}")

        def proj_fm(t, slot_lo, out_sb, out_key, tagp):
            for ob in range(PB):
                bank = [ps0, ps1][ob % 2]

                def mm(tn, ob=ob, bank=bank):
                    last = None
                    for kc in range(KC):
                        si = slot_lo if kc < KC // 2 else slot_lo + 1
                        wv_ = wslot[si][:].rearrange("p (k o) -> p k o", k=KC // 2)
                        wap = wv_[:, kc % (KC // 2), ob * 128:(ob + 1) * 128]
                        last = tn.matmul(bank[:], wap, xn_sb[:, kc, :],
                                         start=(kc == 0), stop=(kc == KC - 1))
                    return last
                OP("tensor", mm, reads=["xn", ("wslot", slot_lo), ("wslot", slot_lo + 1)],
                   writes=[("psb", ob % 2)], tag=f"{tagp}mm{t}_{ob}")

                def ev(s, ob=ob, bank=bank, out_sb=out_sb):
                    return s.copy(out_sb[:, ob, :], bank[:])
                OP("scalar", ev, reads=[("psb", ob % 2)], writes=[out_key],
                   tag=f"{tagp}ev{t}_{ob}")

        def proj_tm_v(t):
            for tb in range(TB):
                for hf in range(2):
                    i = (tb * 2 + hf) % 2
                    bank = [ps0, ps1][i]

                    def mm(tn, tb=tb, hf=hf, bank=bank):
                        last = None
                        for kc in range(KC):
                            xap = xn_sb[:, kc, tb * 128:(tb + 1) * 128]
                            last = tn.matmul(bank[:], xap,
                                             wv_sb[:, kc, hf * 512:(hf + 1) * 512],
                                             start=(kc == 0), stop=(kc == KC - 1))
                        return last
                    OP("tensor", mm, reads=["xn", "wv"], writes=[("psb", i)],
                       tag=f"vmm{t}_{tb}{hf}")

                    def ev(s, tb=tb, hf=hf, bank=bank):
                        return s.copy(vtm_sb[:, tb, hf * 512:(hf + 1) * 512], bank[:])
                    OP("scalar", ev, reads=[("psb", i)], writes=["vtm"],
                       tag=f"vev{t}_{tb}{hf}")

        def k_transpose(t):
            for tb in range(TB):
                for ob in range(PB):
                    i = (tb * PB + ob) % 2
                    bank = [ps0, ps1][i]

                    def mm(tn, tb=tb, ob=ob, bank=bank):
                        return tn.matmul(bank[:, 0:128],
                                         unA[:, ob, tb * 128:(tb + 1) * 128],
                                         ident_bf[:], start=True, stop=True)
                    OP("tensor", mm, reads=["unA", "consts"], writes=[("psb", i)],
                       tag=f"ktr{t}_{tb}{ob}")

                    def ev(s, tb=tb, ob=ob, bank=bank):
                        return s.mul(unB[:, tb, ob * 128:(ob + 1) * 128], bank[:, 0:128],
                                     gcol[:, tb:tb + 1])
                    OP("scalar", ev, reads=[("psb", i), "gcol"], writes=["unB"],
                       tag=f"ktev{t}_{tb}{ob}")

        def state_update(t):
            for dkb in range(PB):
                def mm(tn, dkb=dkb):
                    last = None
                    for tc in range(TB):
                        kap = unB[:, tc, dkb * 128:(dkb + 1) * 128]
                        tn.matmul(psS[:, 0:512], kap, vtm_sb[:, tc, 0:512],
                                  start=(tc == 0), stop=(tc == TB - 1))
                        last = tn.matmul(psS[:, 512:1024], kap, vtm_sb[:, tc, 512:1024],
                                         start=(tc == 0), stop=(tc == TB - 1))
                    return last
                OP("tensor", mm, reads=["unB", "vtm"], writes=["psS"], tag=f"st{t}_{dkb}")

                def v_up(v, dkb=dkb):
                    return v.scalar_tensor_tensor(s_sb[:, dkb, :], s_sb[:, dkb, :],
                                                  GAMMA ** CT, psS[:], ALU.mult, ALU.add)
                OP("vector", v_up, reads=["psS"], writes=["S"], tag=f"sup{t}_{dkb}")

        def attn_tile(t):
            for kb in range(TB):
                bank = [ps0, ps1][kb % 2]

                def mm(tn, kb=kb, bank=bank):
                    last = None
                    for kc in range(KC):
                        kap = unA[:, kc, kb * 128:(kb + 1) * 128]
                        last = tn.matmul(bank[:], kap, qx_sb[:, kc, :],
                                         start=(kc == 0), stop=(kc == KC - 1))
                    return last
                OP("tensor", mm, reads=["unA", "qx"], writes=[("psb", kb % 2)],
                   tag=f"sct{t}_{kb}")

                def v_m(v, kb=kb, bank=bank):
                    return v.tensor_tensor(sct_sb[:, kb, :], bank[:], d_sb[:, kb, :],
                                           ALU.mult)
                OP("vector", v_m, reads=[("psb", kb % 2), ("dmask", kb)], writes=["sqb"],
                   tag=f"sctm{t}_{kb}")

            gb = bc_ap(growbc, 128, PB, CT)

            def v_qt(v):
                return v.tensor_tensor(qx_sb[:], qx_sb[:], gb, ALU.mult)
            OP("vector", v_qt, reads=["qx", "growbc"], writes=["qx"], tag=f"qt{t}")

            for dvb in range(PB):
                bank = psA[dvb % 4]

                def mm(tn, dvb=dvb, bank=bank):
                    last = None
                    for kb in range(TB):
                        vap = vtm_sb[:, kb, dvb * 128:(dvb + 1) * 128]
                        last = tn.matmul(bank[:], vap, sct_sb[:, kb, :],
                                         start=(kb == 0), stop=False)
                    for kc in range(KC):
                        sap = s_sb[:, kc, dvb * 128:(dvb + 1) * 128]
                        last = tn.matmul(bank[:], sap, qx_sb[:, kc, :],
                                         start=False, stop=(kc == KC - 1))
                    return last
                OP("tensor", mm, reads=["sqb", "vtm", "S", "qx"],
                   writes=[("psA", dvb % 4)], tag=f"attn{t}_{dvb}")

                def v_acc(v, dvb=dvb, bank=bank, t=t):
                    return v.tensor_tensor(acc_sb[:, dvb, :], x_sb[t % 2][:, dvb, :],
                                           bank[:], ALU.add)
                OP("vector", v_acc, reads=[("psA", dvb % 4), "x"],
                   writes=[("acc", dvb)], tag=f"acc{t}_{dvb}")

        def routing_tile(t):
            def mm(tn):
                last = None
                for tb in range(TB):
                    for kc in range(KC):
                        xap = xn_sb[:, kc, tb * 128:(tb + 1) * 128]
                        last = tn.matmul(ps0[:, tb * E:(tb + 1) * E], xap,
                                         router_sb[:, kc, :],
                                         start=(kc == 0), stop=(kc == KC - 1))
                return last
            OP("tensor", mm, reads=["xn", "router"], writes=[("psb", 0)], tag=f"rlg{t}")

            def s_ex(s):
                return s.activation(rt_eq[:], ps0[:, 0:TB * E], AF.Exp, scale=-1.0)
            OP("scalar", s_ex, reads=[("psb", 0)], writes=["rt_eq"], tag=f"rex{t}")

            sfc3 = rt_sfc[:].rearrange("p (g e) -> p g e", e=PB)
            eq3 = rt_eq[:].rearrange("p (g e) -> p g e", e=PB)
            m3g = rt_m[:].rearrange("p (g e) -> p g e", e=PB)
            gs3 = rt_gs[:].rearrange("p (tb g) -> p tb g", g=PB)
            gs03 = rt_gs0[:].rearrange("p (tb g) -> p tb g", g=PB)
            g13 = rt_g1[:].rearrange("p (tb g) -> p tb g", g=PB)
            m3t = rt_m[:].rearrange("p (tb e) -> p tb e", e=E)
            sel3 = rt_sel[:].rearrange("p (tb e) -> p tb e", e=E)
            eq3t = rt_eq[:].rearrange("p (tb e) -> p tb e", e=E)
            g1bg = bass.AP(rt_g1.ap().tensor, 0, [[TB * PB, 128], [1, TB * PB], [0, PB]])
            gsbg = bass.AP(rt_gs.ap().tensor, 0, [[TB * PB, 128], [1, TB * PB], [0, PB]])
            r1bg = bass.AP(rt_r1.ap().tensor, 0, [[TB, 128], [1, TB], [0, PB]])
            r1be = bass.AP(rt_r1.ap().tensor, 0, [[TB, 128], [1, TB], [0, E]])

            def v_sig1(v):
                return v.tensor_scalar(rt_eq[:], rt_eq[:], 1.0, None, ALU.add)
            OP("vector", v_sig1, reads=["rt_eq"], writes=["rt_eq"], tag=f"rsg1{t}")

            def v_sig2(v):
                return v.reciprocal(rt_sc[:], rt_eq[:])
            OP("vector", v_sig2, reads=["rt_eq"], writes=["rt_sc"], tag=f"rsg2{t}")

            def v_sig3(v):
                bb = bc_ap(bias_bc, 128, TB, E)
                return v.tensor_tensor(rt_sfc[:], rt_sc[:], bb, ALU.add)
            OP("vector", v_sig3, reads=["rt_sc", "bias_bc"], writes=["rt_sfc"],
               tag=f"rsg3{t}")

            def v_sig4(v):
                return v.reduce_max(rt_gs[:], sfc3, axis=AX.X)
            OP("vector", v_sig4, reads=["rt_sfc"], writes=["rt_gs"], tag=f"rsg4{t}")

            def g_b1(g):
                return g.tensor_copy(rt_g1[:], rt_gs[:])
            OP("gpsimd", g_b1, reads=["rt_gs"], writes=["rt_g1"], tag=f"rb1{t}")

            def v_eq1(v):
                v.tensor_tensor(eq3, sfc3, g1bg, ALU.is_equal)
                return v.nop() if False else v.tensor_copy(rt_sel[:], rt_sfc[:])
            OP("vector", v_eq1, reads=["rt_g1", "rt_sfc"], writes=["rt_eq", "rt_sel"],
               tag=f"req1{t}")

            def g_s1(g):
                return g.tensor_scalar(rt_eq[:], rt_eq[:], NEGBIG, None, ALU.mult)
            OP("gpsimd", g_s1, reads=["rt_eq"], writes=["rt_eq"], tag=f"rs1{t}")

            def v_m2a(v):
                return v.tensor_tensor(rt_m[:], rt_sfc[:], rt_eq[:], ALU.add)
            OP("vector", v_m2a, reads=["rt_eq", "rt_sfc"], writes=["rt_m"],
               tag=f"rm2a{t}")

            def v_m2b(v):
                return v.reduce_max(rt_g1[:], m3g, axis=AX.X)
            OP("vector", v_m2b, reads=["rt_m"], writes=["rt_g1"], tag=f"rm2b{t}")

            def g_gs1(g):
                return g.tensor_tensor(rt_gs[:], rt_gs[:], rt_g1[:], ALU.add)
            OP("gpsimd", g_gs1, reads=["rt_g1", "rt_gs"], writes=["rt_gs"], tag=f"rgs1{t}")

            def g_gs2(g):
                return g.tensor_copy(rt_gs0[:], rt_gs[:])
            OP("gpsimd", g_gs2, reads=["rt_gs"], writes=["rt_gs0"], tag=f"rgs2{t}")

            for it in range(4):
                def v_red(v):
                    return v.reduce_max(rt_r2[:], gs3, axis=AX.X)
                OP("vector", v_red, reads=["rt_gs"], writes=["rt_r2"], tag=f"rt4r{t}_{it}")

                def g_bnc(g):
                    return g.tensor_copy(rt_r1[:], rt_r2[:])
                OP("gpsimd", g_bnc, reads=["rt_r2"], writes=["rt_r1"], tag=f"rt4b{t}_{it}")
                if it < 3:
                    def v_eq(v):
                        return v.tensor_tensor(g13, gs3, r1bg, ALU.is_equal)
                    OP("vector", v_eq, reads=["rt_r1", "rt_gs"], writes=["rt_g1"],
                       tag=f"rt4e{t}_{it}")

                    def g_m1(g):
                        return g.tensor_scalar(rt_g1[:], rt_g1[:], NEGBIG, None, ALU.mult)
                    OP("gpsimd", g_m1, reads=["rt_g1"], writes=["rt_g1"],
                       tag=f"rt4m1{t}_{it}")

                    def g_m2(g):
                        return g.tensor_tensor(rt_gs[:], rt_gs[:], rt_g1[:], ALU.add)
                    OP("gpsimd", g_m2, reads=["rt_g1", "rt_gs"], writes=["rt_gs"],
                       tag=f"rt4m2{t}_{it}")

            def v_gmask(v):
                return v.tensor_tensor(g13, gs03, r1bg, ALU.is_ge)
            OP("vector", v_gmask, reads=["rt_r1", "rt_gs0"], writes=["rt_g1"],
               tag=f"rgm{t}")

            def g_bm(g):
                return g.tensor_copy(rt_gs[:], rt_g1[:])
            OP("gpsimd", g_bm, reads=["rt_g1"], writes=["rt_gs"], tag=f"rbm{t}")

            def v_emask(v):
                return v.tensor_scalar(eq3, gsbg, 0.5, None, ALU.is_le)
            OP("vector", v_emask, reads=["rt_gs"], writes=["rt_eq"], tag=f"rem{t}")

            def g_s2(g):
                return g.tensor_scalar(rt_eq[:], rt_eq[:], NEGBIG, None, ALU.mult)
            OP("gpsimd", g_s2, reads=["rt_eq"], writes=["rt_eq"], tag=f"rs2{t}")

            def v_mask8a(v):
                return v.tensor_tensor(rt_m[:], rt_sfc[:], rt_eq[:], ALU.add)
            OP("vector", v_mask8a, reads=["rt_eq", "rt_sfc"], writes=["rt_m"],
               tag=f"rm8a{t}")

            def v_mask8b(v):
                return v.tensor_copy(rt_sel[:], rt_m[:])
            OP("vector", v_mask8b, reads=["rt_m"], writes=["rt_sel"], tag=f"rm8b{t}")

            for it in range(8):
                def v_red(v):
                    return v.reduce_max(rt_r2[:], m3t, axis=AX.X)
                OP("vector", v_red, reads=["rt_m"], writes=["rt_r2"], tag=f"rt8r{t}_{it}")

                def g_bnc(g):
                    return g.tensor_copy(rt_r1[:], rt_r2[:])
                OP("gpsimd", g_bnc, reads=["rt_r2"], writes=["rt_r1"], tag=f"rt8b{t}_{it}")
                if it < 7:
                    def v_eq(v):
                        return v.tensor_tensor(eq3t, m3t, r1be, ALU.is_equal)
                    OP("vector", v_eq, reads=["rt_r1", "rt_m"], writes=["rt_eq"],
                       tag=f"rt8e{t}_{it}")

                    def g_m(g):
                        return g.tensor_scalar(rt_eq[:], rt_eq[:], NEGBIG, None, ALU.mult)
                    OP("gpsimd", g_m, reads=["rt_eq"], writes=["rt_eq"],
                       tag=f"rt8s{t}_{it}")

                    def v_m(v):
                        return v.tensor_tensor(rt_m[:], rt_m[:], rt_eq[:], ALU.add)
                    OP("vector", v_m, reads=["rt_eq", "rt_m"], writes=["rt_m"],
                       tag=f"rt8a{t}_{it}")

            def v_sel(v):
                return v.tensor_tensor(sel3, sel3, r1be, ALU.is_ge)
            OP("vector", v_sel, reads=["rt_r1", "rt_sel"], writes=["rt_sel"],
               tag=f"rsel{t}")

            def v_twa(v):
                return v.tensor_tensor(rt_m[:], rt_sc[:], rt_sel[:], ALU.mult)
            OP("vector", v_twa, reads=["rt_sel", "rt_sc"], writes=["rt_m"],
               tag=f"rtwa{t}")

            def v_twb(v):
                return v.reduce_sum(rt_r2[:], m3t, axis=AX.X)
            OP("vector", v_twb, reads=["rt_m"], writes=["rt_r2"], tag=f"rtwb{t}")

            def g_den(g):
                return g.tensor_scalar(rt_r1[:], rt_r2[:], 1e-20, None, ALU.add)
            OP("gpsimd", g_den, reads=["rt_r2"], writes=["rt_r1"], tag=f"rden{t}")

            def v_rec(v):
                return v.reciprocal(rt_r2[:], rt_r1[:])
            OP("vector", v_rec, reads=["rt_r1"], writes=["rt_r2"], tag=f"rrec{t}")

            def g_rdc(g):
                return g.tensor_copy(rt_r1[:], rt_r2[:])
            OP("gpsimd", g_rdc, reads=["rt_r2"], writes=["rt_r1"], tag=f"rrdc{t}")

            def v_comba(v):
                return v.scalar_tensor_tensor(m3t, m3t, MOE_SCALE, r1be, ALU.mult,
                                              ALU.mult)
            OP("vector", v_comba, reads=["rt_r1", "rt_m"], writes=["rt_m"],
               tag=f"rcba{t}")

            def v_combb(v):
                return v.tensor_copy(rt_mb[:], rt_m[:])
            OP("vector", v_combb, reads=["rt_m"], writes=["rt_mb"], tag=f"rcbb{t}")

        def up_dma(t, u):
            iu = u % 2

            def f(g, u=u, iu=iu):
                return g.dma_start(out=wslot[iu][:], in_=up_hbm[u])
            OP("gpsimd", f, writes=[("wslot", iu)], dma="dw%d" % iu, tag=f"wup{t}_{u}")

        def down_dma(t, u):
            idn = 2 + u % 2

            def f(g, u=u, idn=idn):
                return g.dma_start(out=wslot[idn][:], in_=down_hbm[u])
            OP("gpsimd", f, writes=[("wslot", idn)], dma="dw%d" % idn, tag=f"wdn{t}_{u}")

        def ffn_tile(t):
            unBf = unB[:].rearrange("p a b -> p (a b)")
            h_aps = [unA[:, 0:IBLK, :].rearrange("p a b -> p (a b)"),
                     unA[:, IBLK:2 * IBLK, :].rearrange("p a b -> p (a b)"),
                     unBf[:, 0:IBLK * CT], unBf[:, IBLK * CT:2 * IBLK * CT]]
            h_keys = ["unA", "unA", "unB", "unB"]

            for g0 in range(0, NUNIT, EG):
                units = list(range(g0, min(g0 + EG, NUNIT)))
                for j, u in enumerate(units):
                    if u + 1 < NUNIT:
                        up_dma(t, u + 1)
                    if j < 2:
                        down_dma(t, u)
                    routed = u < E
                    iu = u % 2
                    if routed:
                        cbank = psA[2 + u % 2]
                        ckey = ("psA", 2 + u % 2)

                        def mm_c(tn, u=u, cbank=cbank):
                            last = None
                            for tb in range(TB):
                                col = bass.AP(rt_mb.ap().tensor, tb * E + u,
                                              [[TB * E, 128], [0, 128]])
                                last = tn.matmul(cbank[:, tb * 128:(tb + 1) * 128],
                                                 col, ident_bf[:],
                                                 start=True, stop=True)
                            return last
                        OP("tensor", mm_c, reads=["rt_mb", "consts"], writes=[ckey],
                           tag=f"cmm{t}_{u}")

                        def ev_c(s, u=u, cbank=cbank):
                            return s.copy(c_sb[u % 2][:], cbank[:])
                        OP("scalar", ev_c, reads=[ckey], writes=[("c", u % 2)],
                           tag=f"cev{t}_{u}")
                    upv = wslot[iu][:].rearrange("p (k i) -> p k i", k=KC)
                    for ib in range(IBLK):
                        bank = [ps0, ps1][ib % 2]

                        def mm(tn, ib=ib, bank=bank, upv=upv):
                            last = None
                            for kc in range(KC):
                                wap = upv[:, kc, ib * 128:(ib + 1) * 128]
                                last = tn.matmul(bank[:], wap, xn_sb[:, kc, :],
                                                 start=(kc == 0), stop=(kc == KC - 1))
                            return last
                        OP("tensor", mm, reads=["xn", ("wslot", iu)],
                           writes=[("psb", ib % 2)], tag=f"up{t}_{u}_{ib}")

                        def s_relu(s, ib=ib, bank=bank):
                            return s.activation(wb1[:, ib, :], bank[:], AF.Relu)
                        OP("scalar", s_relu, reads=[("psb", ib % 2)], writes=["wb1"],
                           tag=f"re{t}_{u}_{ib}")

                    if debug and t == NPRE and u == 0:
                        tap("ffn_relu0", wb1[:, 0:IBLK, :], ["wb1"])
                        tap("ffn_c0", c_sb[0][:], [("c", 0)])
                    def g_sq(g):
                        a = wb1[:, 0:IBLK, :].rearrange("p a b -> p (a b)")
                        return g.tensor_tensor(sqb[:].rearrange("p a b -> p (a b)"),
                                               a, a, ALU.mult)
                    OP("gpsimd", g_sq, reads=["wb1"], writes=["sqb"], tag=f"sq{t}_{u}")
                    if debug and t == NPRE and u == 0:
                        tap("ffn_sq0", sqb[:], ["sqb"])

                    if routed:
                        cbb = bc_ap(c_sb[u % 2], 128, IBLK, CT)

                        def v_h(v, j=j, cbb=cbb):
                            return v.tensor_tensor(h_aps[j],
                                                   sqb[:].rearrange("p a b -> p (a b)"),
                                                   cbb, ALU.mult)
                        OP("vector", v_h, reads=["sqb", ("c", u % 2)],
                           writes=[h_keys[j]], tag=f"h{t}_{u}")
                    else:
                        def v_h(v, j=j):
                            return v.tensor_copy(h_aps[j],
                                                 sqb[:].rearrange("p a b -> p (a b)"))
                        OP("vector", v_h, reads=["sqb"], writes=[h_keys[j]],
                           tag=f"h{t}_{u}")
                for j, u in enumerate(units):
                    idn = 2 + u % 2

                    def mm(tn, j=j, u=u, idn=idn, first=(j == 0), last_u=(j == len(units) - 1)):
                        dnv = wslot[idn][:].rearrange("p (c h) -> p c h", c=ICH)
                        hap = h_aps[j]
                        last = None
                        for hb in range(HB):
                            for ic in range(ICH):
                                wap = dnv[:, ic, hb * 128:(hb + 1) * 128]
                                last = tn.matmul(dn_banks[hb], wap,
                                                 hap[:, ic * CT:(ic + 1) * CT],
                                                 start=(first and ic == 0),
                                                 stop=(last_u and ic == ICH - 1))
                        return last
                    OP("tensor", mm,
                       reads=[h_keys[j], ("wslot", idn)],
                       writes=list(set(dn_keys)), tag=f"dn{t}_{u}")
                    if j + 2 < len(units):
                        down_dma(t, units[j + 2])
                for hb in range(HB):
                    def v_fl(v, hb=hb):
                        return v.tensor_tensor(acc_sb[:, hb, :], acc_sb[:, hb, :],
                                               dn_banks[hb], ALU.add)
                    OP("vector", v_fl, reads=[dn_keys[hb], ("acc", hb)],
                       writes=[("acc", hb)], tag=f"fl{t}_{g0}_{hb}")
                if debug and t == NPRE and g0 == 0:
                    tap("ffn_h0", unA[:], ["unA"])
                    tap("ffn_acc0", acc_sb[:], [("acc", hb2) for hb2 in range(PB)], dt=F32)

        def out_tile(t):
            def f(sy, t=t):
                return sy.dma_start(out=out_hbm[t - NPRE], in_=acc_sb[:])
            OP("sync", f, reads=[("acc", dvb) for dvb in range(PB)], writes=["outdma"],
               dma="dout", tag=f"out{t}")

        def dma_w2(t, src, lo, tag):
            def f0(g):
                return g.dma_start(out=wslot[lo][:], in_=src[:, 0:KC * IM])
            OP("gpsimd", f0, writes=[("wslot", lo)], dma="dw%d" % lo, tag=f"{tag}a{t}")

            def f1(g):
                return g.dma_start(out=wslot[lo + 1][:], in_=src[:, KC * IM:KC * H])
            OP("gpsimd", f1, writes=[("wslot", lo + 1)], dma="dw%d" % (lo + 1),
               tag=f"{tag}b{t}")

        for t in range(NTILE):
            own = t >= NPRE
            dma_w2(t, wk_hbm, 2, "wk")
            load_x(t)
            pos_tile(t)
            ln_tile(t)
            sin_tables(t)
            proj_fm(t, 2, unA, "unA", "k")
            xpos_apply(t, -1.0, unA, "unA")
            proj_tm_v(t)
            if t == NPRE - 1:
                tap("ktm11", unB[:], ["unB"])
                tap("vtm11", vtm_sb[:], ["vtm"])
            if own:
                if t == NPRE:
                    tap("gcol", gcol[:], ["gcol"], dt=F32)
                    tap("invf", invf_row[:], ["featrows"], dt=F32)
                    tap("lnsc", lnsc_row[:], ["featrows"], dt=F32)
                    tap("tabC", tabC[:], ["tabC"])
                    tap("tabS", tabS[:], ["tabS"])
                    tap("alt", altcol[:], ["alt"], dt=F32)
                    tap("xn", xn_sb[:], ["xn"])
                    tap("kx", unA[:], ["unA"])
                    tap("vtm", vtm_sb[:], ["vtm"])
                    tap("s_in", s_sb[:], ["S"])
                dma_w2(t, wq_hbm, 0, "wq")
                proj_fm(t, 0, qx_sb, "qx", "q")
                xpos_apply(t, 1.0, qx_sb, "qx")
                if t == NPRE:
                    tap("qx", qx_sb[:], ["qx"])
                k_transpose(t)
                attn_tile(t)
                if t == NPRE:
                    tap("sct", sqb[:], ["sqb"])
                    tap("acc_attn", acc_sb[:], [("acc", dvb) for dvb in range(PB)], dt=F32)
                state_update(t)
                routing_tile(t)
                if t == NPRE:
                    tap("comb", rt_m[:], ["rt_m", "rt_mb"], dt=F32)
                    tap("rtsc", rt_sc[:], ["rt_sc"], dt=F32)
                    tap("rtsfc", rt_sfc[:], ["rt_sfc"], dt=F32)
                    tap("rtgs", rt_gs[:], ["rt_gs"], dt=F32)
                    tap("rtg1", rt_g1[:], ["rt_gmask"], dt=F32)
                    tap("rtsel", rt_sel[:], ["rt_sel"], dt=F32)
                up_dma(t, 0)
                ffn_tile(t)
                if t == NPRE:
                    tap("acc_ffn", acc_sb[:], [("acc", dvb) for dvb in range(PB)], dt=F32)
                out_tile(t)
            else:
                k_transpose(t)
                state_update(t)

        def f_fin(sy):
            return sy.nop()
        OP("sync", f_fin, reads=["outdma"], tag="final_wait")

        with nc.Block() as block:
            sch.build(block, sems, waitall=("dst",))

    return nc


def _pack_fm(w):
    R, Cc = w.shape
    return np.ascontiguousarray(
        w.reshape(R // 128, 128, Cc).transpose(1, 0, 2).reshape(128, (R // 128) * Cc))


def kernel(L0_attn_X, L0_attn_W_Q, L0_attn_W_K, L0_attn_W_V, L0_ffn_router_w,
           L0_ffn_correction_bias, L0_ffn_up_w, L0_ffn_down_w,
           L0_ffn_shared_up_w, L0_ffn_shared_down_w):
    X = np.asarray(L0_attn_X, dtype=np.float32)
    wq = _pack_fm(np.asarray(L0_attn_W_Q, dtype=np.float32))
    wk = _pack_fm(np.asarray(L0_attn_W_K, dtype=np.float32))
    wv = _pack_fm(np.asarray(L0_attn_W_V, dtype=np.float32))
    router = _pack_fm(np.ascontiguousarray(
        np.asarray(L0_ffn_router_w, dtype=np.float32).T))
    bias = np.ascontiguousarray(
        np.asarray(L0_ffn_correction_bias, dtype=np.float32).reshape(1, E))
    up_w = np.asarray(L0_ffn_up_w, dtype=np.float32)
    down_w = np.asarray(L0_ffn_down_w, dtype=np.float32)
    sup = np.asarray(L0_ffn_shared_up_w, dtype=np.float32)
    sdown = np.asarray(L0_ffn_shared_down_w, dtype=np.float32)

    up_all = np.empty((NUNIT, 128, KC * IM), dtype=np.float32)
    down_all = np.empty((NUNIT, 128, ICH * H), dtype=np.float32)
    for e in range(E):
        up_all[e] = _pack_fm(up_w[e])
        down_all[e] = _pack_fm(down_w[e])
    for s in range(NSH):
        up_all[E + s] = _pack_fm(sup[:, s * IM:(s + 1) * IM])
        down_all[E + s] = _pack_fm(sdown[s * IM:(s + 1) * IM, :])

    nc = build_program()
    pair_row = (np.arange(H, dtype=np.int64) // 2).astype(np.float32).reshape(1, H)
    alt_col = np.where(np.arange(128) % 2 == 0, -1.0, 1.0).astype(np.float32).reshape(128, 1)

    in_maps = []
    for c in range(NCORE):
        b = c // CORES_PER_SEQ
        s0 = (c % CORES_PER_SEQ) * OWN
        xt = np.zeros((NTILE, 128, PB * CT), dtype=np.float32)
        lo = s0 - PRE
        for t in range(NTILE):
            a = lo + t * CT
            if a + CT <= 0:
                continue
            seg = X[b, max(a, 0):a + CT, :]
            n = seg.shape[0]
            fm = seg.T.reshape(PB, 128, n).transpose(1, 0, 2)
            buf = np.zeros((128, PB, CT), dtype=np.float32)
            buf[:, :, CT - n:] = fm
            xt[t] = buf.reshape(128, PB * CT)
        base = np.array([[float(s0 - PRE)]], dtype=np.float32)
        in_maps.append({
            "x_hbm": xt, "wq_hbm": wq, "wk_hbm": wk, "wv_hbm": wv,
            "router_hbm": router, "bias_hbm": bias, "up_hbm": up_all,
            "down_hbm": down_all, "base_hbm": base,
            "pair_hbm": pair_row, "altc_hbm": alt_col,
        })

    res = run_bass_kernel_spmd(nc, in_maps, core_ids=list(range(NCORE)))

    out = np.empty((B, L, H), dtype=np.float32)
    for c in range(NCORE):
        b = c // CORES_PER_SEQ
        s0 = (c % CORES_PER_SEQ) * OWN
        o = np.asarray(res.results[c]["out_hbm"]).reshape(NOWN, 128, PB, CT)
        tm = o.transpose(0, 3, 2, 1).reshape(NOWN * CT, H)
        out[b, s0:s0 + OWN, :] = tm
    return out


# version 10 (final)
# speedup vs baseline: 1.4407x; 1.3979x over previous
from contextlib import ExitStack
import math
import numpy as np
import concourse.bass as bass
import concourse.mybir as mybir
from concourse.bass_utils import run_bass_kernel_spmd

F32 = mybir.dt.float32
BF16 = mybir.dt.bfloat16
I32 = mybir.dt.int32
AF = mybir.ActivationFunctionType
ALU = mybir.AluOpType
AX = mybir.AxisListType

B, L, H = 2, 8192, 1024
NCORE = 8
CORES_PER_SEQ = 4
OWN = L // CORES_PER_SEQ
CT = 512
NOWN = OWN // CT
NW = 3
NPRE = NW - 1
PRE = NPRE * CT
NTILE = NPRE + NOWN
E = 64
IM = 512
ISH = 2048
NSH = ISH // IM
NUNIT = E + NSH
EG = 4
GAMMA = 31.0 / 32.0
LNG = math.log(GAMMA)
SCALE_BASE = 512.0
MOE_SCALE = 2.5
EPS = 1e-5
TWO_PI = 2.0 * math.pi
INV_2PI = 1.0 / (2.0 * math.pi)
PI_HALF = math.pi / 2.0
CLS = 0.49999994
NEGBIG = -1.0e30
PB = H // 128
KC = PB
IBLK = IM // 128
ICH = IM // 128
HB = PB
TB = CT // 128

ENGINES = ("tensor", "vector", "scalar", "gpsimd", "sync")


class _Op:
    __slots__ = ("engine", "fn", "deps", "need_inc", "inc_val", "dma", "tag")

    def __init__(self, engine, fn, dma, tag):
        self.engine, self.fn, self.dma, self.tag = engine, fn, dma, tag
        self.deps = []
        self.need_inc = False
        self.inc_val = None


class Sched:
    def __init__(self):
        self.per_engine = {e: [] for e in ENGINES}
        self.res = {}
        self.ops = []

    def op(self, engine, fn, reads=(), writes=(), dma=None, tag=""):
        o = _Op(engine, fn, dma, tag)
        deps = []
        for r in reads:
            st = self.res.get(r)
            if st and st["w"] is not None:
                deps.append(st["w"])
        for w in writes:
            st = self.res.get(w)
            if st:
                if st["w"] is not None:
                    deps.append(st["w"])
                deps.extend(st["r"])
        seen = set()
        for d in deps:
            if id(d) in seen:
                continue
            seen.add(id(d))
            if (not d.dma) and d.engine == engine == "tensor":
                continue
            o.deps.append(d)
        for r in reads:
            self.res.setdefault(r, {"w": None, "r": []})["r"].append(o)
        for w in writes:
            self.res[w] = {"w": o, "r": []}
        self.ops.append(o)
        self.per_engine[engine].append(o)
        return o

    def build(self, block, sems, waitall=()):
        for o in self.ops:
            for d in o.deps:
                d.need_inc = True
        cum = {}

        def semkey(o):
            return o.dma if o.dma else o.engine

        for e in ENGINES:
            for o in self.per_engine[e]:
                if o.dma:
                    o.need_inc = True
                if o.need_inc:
                    k = semkey(o)
                    cum[k] = cum.get(k, 0) + (16 if o.dma else 1)
                    o.inc_val = cum[k]
        for e in ENGINES:
            for o in self.per_engine[e]:
                if o.dma in waitall:
                    o.inc_val = cum[o.dma]

        def emit_for(engine, eng):
            waited = {}
            for o in self.per_engine[engine]:
                for d in o.deps:
                    k = semkey(d)
                    v = d.inc_val
                    if waited.get(k, 0) >= v:
                        continue
                    eng.wait_ge(sems[k], v)
                    waited[k] = v
                ins = o.fn(eng)
                if o.need_inc:
                    assert ins is not None, f"op {o.tag} returned no instruction"
                    ins.then_inc(sems[semkey(o)], 16 if o.dma else 1)

        block.tensor(lambda eng: emit_for("tensor", eng))
        block.vector(lambda eng: emit_for("vector", eng))
        block.scalar(lambda eng: emit_for("scalar", eng))
        block.gpsimd(lambda eng: emit_for("gpsimd", eng))
        block.sync(lambda eng: emit_for("sync", eng))
        return cum


def bc_ap(sbt, n_p, rep, inner, off=0):
    return bass.AP(sbt.ap().tensor, off, [[inner, n_p], [0, rep], [1, inner]])


def build_program(debug=False):
    nc = bass.Bass("TRN2", target_bir_lowering=False)
    dr = lambda name, shape, dt, kind: nc.dram_tensor(name, shape, dt, kind=kind).ap()
    x_hbm = dr("x_hbm", [NTILE, 128, PB * CT], F32, "ExternalInput")
    wq_hbm = dr("wq_hbm", [128, KC * H], F32, "ExternalInput")
    wk_hbm = dr("wk_hbm", [128, KC * H], F32, "ExternalInput")
    wv_hbm = dr("wv_hbm", [128, KC * H], F32, "ExternalInput")
    router_hbm = dr("router_hbm", [128, KC * E], F32, "ExternalInput")
    bias_hbm = dr("bias_hbm", [1, E], F32, "ExternalInput")
    up_hbm = dr("up_hbm", [NUNIT, 128, KC * IM], F32, "ExternalInput")
    down_hbm = dr("down_hbm", [NUNIT, 128, ICH * H], F32, "ExternalInput")
    tabk_hbm = dr("tabk_hbm", [NTILE, 128, 2 * PB * CT], F32, "ExternalInput")
    tabq_hbm = dr("tabq_hbm", [NOWN, 128, 2 * PB * CT], F32, "ExternalInput")
    d0_hbm = dr("d0_hbm", [128, TB * CT], F32, "ExternalInput")
    dx_hbm = dr("dx_hbm", [128, (NW - 1) * TB * CT], F32, "ExternalInput")
    out_hbm = dr("out_hbm", [NOWN, 128, PB * CT], F32, "ExternalOutput")

    sch = Sched()
    es = ExitStack()
    with es:
        def sb(n, sh, dt):
            return es.enter_context(nc.sbuf_tensor(n, sh, dt))

        def psb(n, sh):
            return es.enter_context(nc.psum_tensor(n, sh, F32))

        router_sb = sb("router_sb", [128, KC, E], BF16)
        ident_bf = sb("ident_bf", [128, 128], BF16)
        bias_bc = sb("bias_bc", [128, E], F32)
        rows = sb("rows", [1, 4 * CT], F32)
        bias_row = sb("bias_row", [1, E], F32)
        onesf_row = sb("onesf_row", [1, 128], F32)
        d0_sb = sb("d0_sb", [128, TB, CT], BF16)
        dx_sb = sb("dx_sb", [128, (NW - 1) * TB, CT], BF16)
        _x1 = sb("x_sb", [128, PB, CT], BF16)
        x_sb = [_x1, _x1]
        xn_sb = sb("xn_sb", [128, PB, CT], BF16)
        wb1 = sb("wb1", [128, IBLK, CT], BF16)
        sqb = sb("sqb", [128, IBLK, CT], BF16)
        qx_sb = sb("qx_sb", [128, PB, CT], BF16)
        kxr = [sb(f"kxr{i}", [128, PB, CT], BF16) for i in range(NW)]
        vtr = [sb(f"vtr{i}", [128, TB, H], BF16) for i in range(NW)]
        tab_sb = sb("tab_sb", [128, 2 * PB, CT], BF16)
        acc_sb = sb("acc_sb", [128, PB, CT], F32)
        c_sb = [sb(f"c_sb{i}", [128, CT], BF16) for i in range(2)]
        unA = sb("unA", [128, PB, CT], BF16)
        unB = sb("unB", [128, TB, H], BF16)
        wslot = [sb(f"wslot{i}", [128, KC * IM], BF16) for i in range(4)]
        rt_sc = sb("rt_sc", [128, TB * E], F32)
        rt_sfc = sb("rt_sfc", [128, TB * E], F32)
        rt_m = sb("rt_m", [128, TB * E], F32)
        rt_sel = sb("rt_sel", [128, TB * E], F32)
        rt_eq = sb("rt_eq", [128, TB * E], F32)
        rt_mb = sb("rt_mb", [128, TB * E], BF16)
        rt_gs = sb("rt_gs", [128, TB * PB], F32)
        rt_gs0 = sb("rt_gs0", [128, TB * PB], F32)
        rt_g1 = sb("rt_g1", [128, TB * PB], F32)
        rt_r1 = sb("rt_r1", [128, TB], F32)
        rt_r2 = sb("rt_r2", [128, TB], F32)
        unBf = unB[:].rearrange("p a b -> p (a b)")

        def sct_ap(j):
            return unA[:, j, :] if j < PB else unBf[:, (j - PB) * CT:(j - PB + 1) * CT]

        def sct_key(j):
            return "unA" if j < PB else "unB"

        ps0 = psb("ps0", [128, 512])
        ps1 = psb("ps1", [128, 512])
        psS = psb("psS", [128, 1024])
        psA = [psb(f"psA{i}", [128, 512]) for i in range(4)]
        dn_banks = [ps0[:], ps1[:], psS[:, 0:512], psS[:, 512:1024],
                    psA[0][:], psA[1][:], psA[2][:], psA[3][:]]
        dn_keys = [("psb", 0), ("psb", 1), "psS", "psS",
                   ("psA", 0), ("psA", 1), ("psA", 2), ("psA", 3)]

        sems = {}
        for e in ENGINES:
            sems[e] = es.enter_context(nc.semaphore("sem_" + e))
        for k in ("dst", "dx", "dw0", "dw1", "dw2", "dw3", "dout", "dtap", "dtb"):
            sems[k] = es.enter_context(nc.semaphore("sem_" + k))

        OP = sch.op
        dbg_n = [0]

        def tap(name, ap, reads, dt=BF16):
            if not debug:
                return
            shp = list(ap.shape)
            d = nc.dram_tensor(f"dbg_{name}", shp, dt, kind="ExternalOutput").ap()

            def f(sy, ap=ap, d=d):
                return sy.dma_start(out=d, in_=ap)
            OP("sync", f, reads=list(reads), writes=[f"dbgw_{name}", "tapchain"],
               dma="dtap", tag=f"tap_{name}")
            dbg_n[0] += 1

        ones_bf_ap = nc.const_aps.tensor(1.0, [128, 1], BF16)
        def g_setup(g):
            g.memset(onesf_row[:], 1.0)
            return g.iota(rt_eq[:, 0:128], pattern=[[-1, 128]], channel_multiplier=1,
                          allow_small_or_imprecise_dtypes=True)
        OP("gpsimd", g_setup, writes=["consts", "rt_eq"], tag="setup_g")

        def g_id1(g):
            return g.tensor_scalar(rt_eq[:, 0:128], rt_eq[:, 0:128], 0.0, None,
                                   ALU.is_equal)
        OP("gpsimd", g_id1, reads=["rt_eq"], writes=["rt_eq"], tag="setup_id1")

        def g_id2(g):
            return g.tensor_copy(ident_bf[:], rt_eq[:, 0:128])
        OP("gpsimd", g_id2, reads=["rt_eq"], writes=["consts"], tag="setup_id2")

        def mk_dma(dst, src, wkey, tag):
            def f(g):
                return g.dma_start(out=dst, in_=src)
            OP("gpsimd", f, writes=[wkey], dma="dst", tag=tag)

        mk_dma(router_sb[:], router_hbm, "router", "dma_router")
        mk_dma(bias_row[:], bias_hbm, "bias_r", "dma_bias")
        mk_dma(d0_sb[:], d0_hbm, "dlut0", "dma_d0")
        mk_dma(dx_sb[:], dx_hbm, "dlutx", "dma_dx")

        def mm_biasbc(tn):
            return tn.matmul(ps1[:, 0:E], onesf_row[:], bias_row[:],
                             start=True, stop=True)
        OP("tensor", mm_biasbc, reads=["bias_r", "consts"], writes=[("psb", 1)],
           tag="setup_biasbc1")

        def s_biasbc(s):
            return s.copy(bias_bc[:], ps1[:, 0:E])
        OP("scalar", s_biasbc, reads=[("psb", 1)], writes=["bias_bc"], tag="setup_biasbc2")

        def load_x(t):
            def f(g, t=t):
                return g.dma_start(out=x_sb[t % 2][:], in_=x_hbm[t])
            OP("gpsimd", f, writes=["x"], dma="dx", tag=f"ldx{t}")

        def tab_dma(src, tag):
            def f(g, src=src):
                return g.dma_start(out=tab_sb[:], in_=src)
            OP("gpsimd", f, writes=["tab"], dma="dtb", tag=tag)

        def ln_tile(t):
            xb = x_sb[t % 2]

            def mm_sum(tn, xb=xb):
                last = None
                for pb_ in range(PB):
                    last = tn.matmul(ps0[0:1, 0:CT], ones_bf_ap, xb[:, pb_, :],
                                     start=(pb_ == 0), stop=(pb_ == PB - 1))
                return last
            OP("tensor", mm_sum, reads=["x"], writes=[("psb", 0)], tag=f"lnsum{t}")

            for hf in range(2):
                def s_sq(s, xb=xb, hf=hf):
                    return s.activation(wb1[:], xb[:, hf * IBLK:(hf + 1) * IBLK, :],
                                        AF.Square)
                OP("scalar", s_sq, reads=["x"], writes=["wb1"], tag=f"lnsq{t}_{hf}")

                def mm_sq(tn, hf=hf):
                    last = None
                    for pb_ in range(IBLK):
                        last = tn.matmul(ps1[0:1, 0:CT], ones_bf_ap, wb1[:, pb_, :],
                                         start=(hf == 0 and pb_ == 0),
                                         stop=(hf == 1 and pb_ == IBLK - 1))
                    return last
                OP("tensor", mm_sq, reads=["wb1"], writes=[("psb", 1)],
                   tag=f"lnsum2_{t}_{hf}")

            MEAN, WORK, RSTD, MR = 0, 1, 2, 3
            rr = lambda i: rows[0:1, i * CT:(i + 1) * CT]

            def v_rows(v):
                v.tensor_scalar(rr(MEAN), ps0[0:1, 0:CT], 1.0 / H, None, ALU.mult)
                return v.tensor_scalar(rr(WORK), ps1[0:1, 0:CT], 1.0 / H, None,
                                       ALU.mult)
            OP("vector", v_rows, reads=[("psb", 0), ("psb", 1)], writes=["rows"],
               tag=f"lnrows{t}")

            def v_rows2(v):
                return v.tensor_tensor(rr(RSTD), rr(MEAN), rr(MEAN), ALU.mult)
            OP("vector", v_rows2, reads=["rows"], writes=["rows"], tag=f"lnrsq{t}")

            def v_rows3(v):
                return v.scalar_tensor_tensor(rr(WORK), rr(WORK), EPS, rr(RSTD),
                                              ALU.add, ALU.subtract)
            OP("vector", v_rows3, reads=["rows"], writes=["rows"], tag=f"lnvar{t}")

            def s_rstd(s):
                return s.activation(rr(WORK), rr(WORK), AF.Ln)
            OP("scalar", s_rstd, reads=["rows"], writes=["rows"], tag=f"lnrstd{t}")

            def s_rstd2(s):
                return s.activation(rr(RSTD), rr(WORK), AF.Exp, scale=-0.5)
            OP("scalar", s_rstd2, reads=["rows"], writes=["rows"], tag=f"lnrstd2{t}")

            def v_mr(v):
                return v.tensor_tensor(rr(MR), rr(MEAN), rr(RSTD), ALU.mult)
            OP("vector", v_mr, reads=["rows"], writes=["rows"], tag=f"lnmr{t}")

            def mm_bc(tn):
                tn.matmul(ps0[:, 0:CT], onesf_row[:], rr(RSTD), start=True, stop=True)
                return tn.matmul(ps1[:, 0:CT], onesf_row[:], rr(MR), start=True,
                                 stop=True)
            OP("tensor", mm_bc, reads=["rows", "consts"],
               writes=[("psb", 0), ("psb", 1)], tag=f"lnbc{t}")

            rb = bass.AP(ps0.ap().tensor, 0, [[512, 128], [0, PB], [1, CT]])
            mb = bass.AP(ps1.ap().tensor, 0, [[512, 128], [0, PB], [1, CT]])

            def v_xn(v, xb=xb, rb=rb):
                return v.tensor_tensor(xn_sb[:], xb[:], rb, ALU.mult)
            OP("vector", v_xn, reads=["x", ("psb", 0)], writes=["xn"], tag=f"xn{t}")

            def v_xn2(v, mb=mb):
                return v.tensor_tensor(xn_sb[:], xn_sb[:], mb, ALU.subtract)
            OP("vector", v_xn2, reads=["xn", ("psb", 1)], writes=["xn"], tag=f"xn2{t}")

        def xpos_apply(t, buf_sb, buf_key):
            mask = [i ^ 1 for i in range(32)]
            for hf in range(2):
                def v_sh(v, hf=hf, buf_sb=buf_sb):
                    return v.stream_shuffle(
                        wb1[:].rearrange("p a b -> p (a b)"),
                        buf_sb[:, hf * IBLK:(hf + 1) * IBLK, :].rearrange(
                            "p a b -> p (a b)"), mask)
                OP("vector", v_sh, reads=[buf_key], writes=["wb1"],
                   tag=f"xsh{t}{buf_key}{hf}")

                for k in range(IBLK):
                    pbi = hf * IBLK + k

                    def g_t2(g, k=k, pbi=pbi):
                        return g.tensor_tensor(sqb[:, k, :], wb1[:, k, :],
                                               tab_sb[:, PB + pbi, :], ALU.mult)
                    OP("gpsimd", g_t2, reads=["wb1", "tab"], writes=["sqb"],
                       tag=f"xt2{t}{buf_key}{pbi}")

                    def v_f1(v, pbi=pbi, buf_sb=buf_sb):
                        return v.tensor_tensor(buf_sb[:, pbi, :], buf_sb[:, pbi, :],
                                               tab_sb[:, pbi, :], ALU.mult)
                    OP("vector", v_f1, reads=["tab", buf_key], writes=[buf_key],
                       tag=f"xf1{t}{buf_key}{pbi}")

                    def v_f2(v, k=k, pbi=pbi, buf_sb=buf_sb):
                        return v.tensor_tensor(buf_sb[:, pbi, :], buf_sb[:, pbi, :],
                                               sqb[:, k, :], ALU.add)
                    OP("vector", v_f2, reads=["sqb", buf_key], writes=[buf_key],
                       tag=f"xf2{t}{buf_key}{pbi}")

        def proj_fm(t, slot_lo, out_sb, out_key, tagp):
            for ob in range(PB):
                bank = [ps0, ps1][ob % 2]

                def mm(tn, ob=ob, bank=bank):
                    last = None
                    for kc in range(KC):
                        si = slot_lo if kc < KC // 2 else slot_lo + 1
                        wv_ = wslot[si][:].rearrange("p (k o) -> p k o", k=KC // 2)
                        wap = wv_[:, kc % (KC // 2), ob * 128:(ob + 1) * 128]
                        last = tn.matmul(bank[:], wap, xn_sb[:, kc, :],
                                         start=(kc == 0), stop=(kc == KC - 1))
                    return last
                OP("tensor", mm, reads=["xn", ("wslot", slot_lo), ("wslot", slot_lo + 1)],
                   writes=[("psb", ob % 2)], tag=f"{tagp}mm{t}_{ob}")

                def ev(s, ob=ob, bank=bank, out_sb=out_sb):
                    return s.copy(out_sb[:, ob, :], bank[:])
                OP("scalar", ev, reads=[("psb", ob % 2)], writes=[out_key],
                   tag=f"{tagp}ev{t}_{ob}")

        def proj_tm_v(t):
            rt = t % NW
            for tb in range(TB):
                for hf in range(2):
                    i = (tb * 2 + hf) % 2
                    bank = [ps0, ps1][i]

                    def mm(tn, tb=tb, hf=hf, bank=bank):
                        last = None
                        for kc in range(KC):
                            sl = wslot[2 if kc < KC // 2 else 3]
                            wv_ = sl[:].rearrange("p (k o) -> p k o", k=KC // 2)
                            xap = xn_sb[:, kc, tb * 128:(tb + 1) * 128]
                            last = tn.matmul(bank[:], xap,
                                             wv_[:, kc % (KC // 2),
                                                 hf * 512:(hf + 1) * 512],
                                             start=(kc == 0), stop=(kc == KC - 1))
                        return last
                    OP("tensor", mm, reads=["xn", ("wslot", 2), ("wslot", 3)],
                       writes=[("psb", i)], tag=f"vmm{t}_{tb}{hf}")

                    def ev(s, tb=tb, hf=hf, bank=bank, rt=rt):
                        return s.copy(vtr[rt][:, tb, hf * 512:(hf + 1) * 512], bank[:])
                    OP("scalar", ev, reads=[("psb", i)], writes=[("vt", rt)],
                       tag=f"vev{t}_{tb}{hf}")

        def attn_tile(t):
            for w in range(NW - 1, -1, -1):
                src = (t - w) % NW
                for kb in range(TB):
                    j = (NW - 1 - w) * TB + kb
                    bank = [ps0, ps1][j % 2]

                    def mm(tn, src=src, kb=kb, bank=bank):
                        last = None
                        for kc in range(KC):
                            kap = kxr[src][:, kc, kb * 128:(kb + 1) * 128]
                            last = tn.matmul(bank[:], kap, qx_sb[:, kc, :],
                                             start=(kc == 0), stop=(kc == KC - 1))
                        return last
                    OP("tensor", mm, reads=[("kx", src), "qx"],
                       writes=[("psb", j % 2)], tag=f"sc{t}_{w}_{kb}")

                    dap = (d0_sb[:, kb, :] if w == 0
                           else dx_sb[:, (w - 1) * TB + kb, :])
                    dkey = "dlut0" if w == 0 else "dlutx"

                    def v_m(v, j=j, bank=bank, dap=dap):
                        return v.tensor_tensor(sct_ap(j), bank[:], dap, ALU.mult)
                    OP("vector", v_m, reads=[("psb", j % 2), dkey],
                       writes=[sct_key(j)], tag=f"scm{t}_{w}_{kb}")
            for dvb in range(PB):
                bank = psA[dvb % 4]

                def mm(tn, dvb=dvb, bank=bank, t=t):
                    last = None
                    for w in range(NW - 1, -1, -1):
                        src = (t - w) % NW
                        for kb in range(TB):
                            j = (NW - 1 - w) * TB + kb
                            vap = vtr[src][:, kb, dvb * 128:(dvb + 1) * 128]
                            last = tn.matmul(bank[:], vap, sct_ap(j),
                                             start=(j == 0),
                                             stop=(j == NW * TB - 1))
                    return last
                OP("tensor", mm,
                   reads=["unA", "unB", ("vt", t % NW), ("vt", (t - 1) % NW),
                          ("vt", (t - 2) % NW)],
                   writes=[("psA", dvb % 4)], tag=f"av{t}_{dvb}")

                def v_acc(v, dvb=dvb, bank=bank, t=t):
                    return v.tensor_tensor(acc_sb[:, dvb, :], x_sb[t % 2][:, dvb, :],
                                           bank[:], ALU.add)
                OP("vector", v_acc, reads=[("psA", dvb % 4), "x"],
                   writes=[("acc", dvb)], tag=f"acc{t}_{dvb}")

        def routing_tile(t):
            def mm(tn):
                last = None
                for tb in range(TB):
                    for kc in range(KC):
                        xap = xn_sb[:, kc, tb * 128:(tb + 1) * 128]
                        last = tn.matmul(ps0[:, tb * E:(tb + 1) * E], xap,
                                         router_sb[:, kc, :],
                                         start=(kc == 0), stop=(kc == KC - 1))
                return last
            OP("tensor", mm, reads=["xn", "router"], writes=[("psb", 0)], tag=f"rlg{t}")

            def s_ex(s):
                return s.activation(rt_eq[:], ps0[:, 0:TB * E], AF.Exp, scale=-1.0)
            OP("scalar", s_ex, reads=[("psb", 0)], writes=["rt_eq"], tag=f"rex{t}")

            sfc3 = rt_sfc[:].rearrange("p (g e) -> p g e", e=PB)
            eq3 = rt_eq[:].rearrange("p (g e) -> p g e", e=PB)
            m3g = rt_m[:].rearrange("p (g e) -> p g e", e=PB)
            gs3 = rt_gs[:].rearrange("p (tb g) -> p tb g", g=PB)
            gs03 = rt_gs0[:].rearrange("p (tb g) -> p tb g", g=PB)
            g13 = rt_g1[:].rearrange("p (tb g) -> p tb g", g=PB)
            m3t = rt_m[:].rearrange("p (tb e) -> p tb e", e=E)
            sel3 = rt_sel[:].rearrange("p (tb e) -> p tb e", e=E)
            eq3t = rt_eq[:].rearrange("p (tb e) -> p tb e", e=E)
            g1bg = bass.AP(rt_g1.ap().tensor, 0, [[TB * PB, 128], [1, TB * PB], [0, PB]])
            gsbg = bass.AP(rt_gs.ap().tensor, 0, [[TB * PB, 128], [1, TB * PB], [0, PB]])
            r1bg = bass.AP(rt_r1.ap().tensor, 0, [[TB, 128], [1, TB], [0, PB]])
            r1be = bass.AP(rt_r1.ap().tensor, 0, [[TB, 128], [1, TB], [0, E]])

            def v_sig1(v):
                return v.tensor_scalar(rt_eq[:], rt_eq[:], 1.0, None, ALU.add)
            OP("vector", v_sig1, reads=["rt_eq"], writes=["rt_eq"], tag=f"rsg1{t}")

            def v_sig2(v):
                return v.reciprocal(rt_sc[:], rt_eq[:])
            OP("vector", v_sig2, reads=["rt_eq"], writes=["rt_sc"], tag=f"rsg2{t}")

            def v_sig3(v):
                bb = bc_ap(bias_bc, 128, TB, E)
                return v.tensor_tensor(rt_sfc[:], rt_sc[:], bb, ALU.add)
            OP("vector", v_sig3, reads=["rt_sc", "bias_bc"], writes=["rt_sfc"],
               tag=f"rsg3{t}")

            def v_sig4(v):
                return v.reduce_max(rt_gs[:], sfc3, axis=AX.X)
            OP("vector", v_sig4, reads=["rt_sfc"], writes=["rt_gs"], tag=f"rsg4{t}")

            def g_b1(g):
                return g.tensor_copy(rt_g1[:], rt_gs[:])
            OP("gpsimd", g_b1, reads=["rt_gs"], writes=["rt_g1"], tag=f"rb1{t}")

            def v_eq1(v):
                v.tensor_tensor(eq3, sfc3, g1bg, ALU.is_equal)
                return v.nop() if False else v.tensor_copy(rt_sel[:], rt_sfc[:])
            OP("vector", v_eq1, reads=["rt_g1", "rt_sfc"], writes=["rt_eq", "rt_sel"],
               tag=f"req1{t}")

            def g_s1(g):
                return g.tensor_scalar(rt_eq[:], rt_eq[:], NEGBIG, None, ALU.mult)
            OP("gpsimd", g_s1, reads=["rt_eq"], writes=["rt_eq"], tag=f"rs1{t}")

            def v_m2a(v):
                return v.tensor_tensor(rt_m[:], rt_sfc[:], rt_eq[:], ALU.add)
            OP("vector", v_m2a, reads=["rt_eq", "rt_sfc"], writes=["rt_m"],
               tag=f"rm2a{t}")

            def v_m2b(v):
                return v.reduce_max(rt_g1[:], m3g, axis=AX.X)
            OP("vector", v_m2b, reads=["rt_m"], writes=["rt_g1"], tag=f"rm2b{t}")

            def g_gs1(g):
                return g.tensor_tensor(rt_gs[:], rt_gs[:], rt_g1[:], ALU.add)
            OP("gpsimd", g_gs1, reads=["rt_g1", "rt_gs"], writes=["rt_gs"], tag=f"rgs1{t}")

            def g_gs2(g):
                return g.tensor_copy(rt_gs0[:], rt_gs[:])
            OP("gpsimd", g_gs2, reads=["rt_gs"], writes=["rt_gs0"], tag=f"rgs2{t}")

            for it in range(4):
                def v_red(v):
                    return v.reduce_max(rt_r2[:], gs3, axis=AX.X)
                OP("vector", v_red, reads=["rt_gs"], writes=["rt_r2"], tag=f"rt4r{t}_{it}")

                def g_bnc(g):
                    return g.tensor_copy(rt_r1[:], rt_r2[:])
                OP("gpsimd", g_bnc, reads=["rt_r2"], writes=["rt_r1"], tag=f"rt4b{t}_{it}")
                if it < 3:
                    def v_eq(v):
                        return v.tensor_tensor(g13, gs3, r1bg, ALU.is_equal)
                    OP("vector", v_eq, reads=["rt_r1", "rt_gs"], writes=["rt_g1"],
                       tag=f"rt4e{t}_{it}")

                    def g_m1(g):
                        return g.tensor_scalar(rt_g1[:], rt_g1[:], NEGBIG, None, ALU.mult)
                    OP("gpsimd", g_m1, reads=["rt_g1"], writes=["rt_g1"],
                       tag=f"rt4m1{t}_{it}")

                    def g_m2(g):
                        return g.tensor_tensor(rt_gs[:], rt_gs[:], rt_g1[:], ALU.add)
                    OP("gpsimd", g_m2, reads=["rt_g1", "rt_gs"], writes=["rt_gs"],
                       tag=f"rt4m2{t}_{it}")

            def v_gmask(v):
                return v.tensor_tensor(g13, gs03, r1bg, ALU.is_ge)
            OP("vector", v_gmask, reads=["rt_r1", "rt_gs0"], writes=["rt_g1"],
               tag=f"rgm{t}")

            def g_bm(g):
                return g.tensor_copy(rt_gs[:], rt_g1[:])
            OP("gpsimd", g_bm, reads=["rt_g1"], writes=["rt_gs"], tag=f"rbm{t}")

            def v_emask(v):
                return v.tensor_scalar(eq3, gsbg, 0.5, None, ALU.is_le)
            OP("vector", v_emask, reads=["rt_gs"], writes=["rt_eq"], tag=f"rem{t}")

            def g_s2(g):
                return g.tensor_scalar(rt_eq[:], rt_eq[:], NEGBIG, None, ALU.mult)
            OP("gpsimd", g_s2, reads=["rt_eq"], writes=["rt_eq"], tag=f"rs2{t}")

            def v_mask8a(v):
                return v.tensor_tensor(rt_m[:], rt_sfc[:], rt_eq[:], ALU.add)
            OP("vector", v_mask8a, reads=["rt_eq", "rt_sfc"], writes=["rt_m"],
               tag=f"rm8a{t}")

            def v_mask8b(v):
                return v.tensor_copy(rt_sel[:], rt_m[:])
            OP("vector", v_mask8b, reads=["rt_m"], writes=["rt_sel"], tag=f"rm8b{t}")

            for it in range(8):
                def v_red(v):
                    return v.reduce_max(rt_r2[:], m3t, axis=AX.X)
                OP("vector", v_red, reads=["rt_m"], writes=["rt_r2"], tag=f"rt8r{t}_{it}")

                def g_bnc(g):
                    return g.tensor_copy(rt_r1[:], rt_r2[:])
                OP("gpsimd", g_bnc, reads=["rt_r2"], writes=["rt_r1"], tag=f"rt8b{t}_{it}")
                if it < 7:
                    def v_eq(v):
                        return v.tensor_tensor(eq3t, m3t, r1be, ALU.is_equal)
                    OP("vector", v_eq, reads=["rt_r1", "rt_m"], writes=["rt_eq"],
                       tag=f"rt8e{t}_{it}")

                    def g_m(g):
                        return g.tensor_scalar(rt_eq[:], rt_eq[:], NEGBIG, None, ALU.mult)
                    OP("gpsimd", g_m, reads=["rt_eq"], writes=["rt_eq"],
                       tag=f"rt8s{t}_{it}")

                    def v_m(v):
                        return v.tensor_tensor(rt_m[:], rt_m[:], rt_eq[:], ALU.add)
                    OP("vector", v_m, reads=["rt_eq", "rt_m"], writes=["rt_m"],
                       tag=f"rt8a{t}_{it}")

            def v_sel(v):
                return v.tensor_tensor(sel3, sel3, r1be, ALU.is_ge)
            OP("vector", v_sel, reads=["rt_r1", "rt_sel"], writes=["rt_sel"],
               tag=f"rsel{t}")

            def v_twa(v):
                return v.tensor_tensor(rt_m[:], rt_sc[:], rt_sel[:], ALU.mult)
            OP("vector", v_twa, reads=["rt_sel", "rt_sc"], writes=["rt_m"],
               tag=f"rtwa{t}")

            def v_twb(v):
                return v.reduce_sum(rt_r2[:], m3t, axis=AX.X)
            OP("vector", v_twb, reads=["rt_m"], writes=["rt_r2"], tag=f"rtwb{t}")

            def g_den(g):
                return g.tensor_scalar(rt_r1[:], rt_r2[:], 1e-20, None, ALU.add)
            OP("gpsimd", g_den, reads=["rt_r2"], writes=["rt_r1"], tag=f"rden{t}")

            def v_rec(v):
                return v.reciprocal(rt_r2[:], rt_r1[:])
            OP("vector", v_rec, reads=["rt_r1"], writes=["rt_r2"], tag=f"rrec{t}")

            def g_rdc(g):
                return g.tensor_copy(rt_r1[:], rt_r2[:])
            OP("gpsimd", g_rdc, reads=["rt_r2"], writes=["rt_r1"], tag=f"rrdc{t}")

            def v_comba(v):
                return v.scalar_tensor_tensor(m3t, m3t, MOE_SCALE, r1be, ALU.mult,
                                              ALU.mult)
            OP("vector", v_comba, reads=["rt_r1", "rt_m"], writes=["rt_m"],
               tag=f"rcba{t}")

            def v_combb(v):
                return v.tensor_copy(rt_mb[:], rt_m[:])
            OP("vector", v_combb, reads=["rt_m"], writes=["rt_mb"], tag=f"rcbb{t}")

        def up_dma(t, u):
            iu = u % 2

            def f(g, u=u, iu=iu):
                return g.dma_start(out=wslot[iu][:], in_=up_hbm[u])
            OP("gpsimd", f, writes=[("wslot", iu)], dma="dw%d" % iu, tag=f"wup{t}_{u}")

        def down_dma(t, u):
            idn = 2 + u % 2

            def f(g, u=u, idn=idn):
                return g.dma_start(out=wslot[idn][:], in_=down_hbm[u])
            OP("gpsimd", f, writes=[("wslot", idn)], dma="dw%d" % idn, tag=f"wdn{t}_{u}")

        def ffn_tile(t):
            unBf = unB[:].rearrange("p a b -> p (a b)")
            h_aps = [unA[:, 0:IBLK, :].rearrange("p a b -> p (a b)"),
                     unA[:, IBLK:2 * IBLK, :].rearrange("p a b -> p (a b)"),
                     unBf[:, 0:IBLK * CT], unBf[:, IBLK * CT:2 * IBLK * CT]]
            h_keys = ["unA", "unA", "unB", "unB"]

            for g0 in range(0, NUNIT, EG):
                units = list(range(g0, min(g0 + EG, NUNIT)))
                for j, u in enumerate(units):
                    if u + 1 < NUNIT:
                        up_dma(t, u + 1)
                    if j < 2:
                        down_dma(t, u)
                    routed = u < E
                    iu = u % 2
                    if routed:
                        cbank = psA[2 + u % 2]
                        ckey = ("psA", 2 + u % 2)

                        def mm_c(tn, u=u, cbank=cbank):
                            last = None
                            for tb in range(TB):
                                col = bass.AP(rt_mb.ap().tensor, tb * E + u,
                                              [[TB * E, 128], [0, 128]])
                                last = tn.matmul(cbank[:, tb * 128:(tb + 1) * 128],
                                                 col, ident_bf[:],
                                                 start=True, stop=True)
                            return last
                        OP("tensor", mm_c, reads=["rt_mb", "consts"], writes=[ckey],
                           tag=f"cmm{t}_{u}")

                        def ev_c(s, u=u, cbank=cbank):
                            return s.copy(c_sb[u % 2][:], cbank[:])
                        OP("scalar", ev_c, reads=[ckey], writes=[("c", u % 2)],
                           tag=f"cev{t}_{u}")
                    upv = wslot[iu][:].rearrange("p (k i) -> p k i", k=KC)
                    for ib in range(IBLK):
                        bank = [ps0, ps1][ib % 2]

                        def mm(tn, ib=ib, bank=bank, upv=upv):
                            last = None
                            for kc in range(KC):
                                wap = upv[:, kc, ib * 128:(ib + 1) * 128]
                                last = tn.matmul(bank[:], wap, xn_sb[:, kc, :],
                                                 start=(kc == 0), stop=(kc == KC - 1))
                            return last
                        OP("tensor", mm, reads=["xn", ("wslot", iu)],
                           writes=[("psb", ib % 2)], tag=f"up{t}_{u}_{ib}")

                        def s_relu(s, ib=ib, bank=bank):
                            return s.activation(wb1[:, ib, :], bank[:], AF.Relu)
                        OP("scalar", s_relu, reads=[("psb", ib % 2)], writes=["wb1"],
                           tag=f"re{t}_{u}_{ib}")

                    if debug and t == NPRE and u == 0:
                        tap("ffn_relu0", wb1[:, 0:IBLK, :], ["wb1"])
                        tap("ffn_c0", c_sb[0][:], [("c", 0)])
                    def g_sq(g):
                        a = wb1[:, 0:IBLK, :].rearrange("p a b -> p (a b)")
                        return g.tensor_tensor(sqb[:].rearrange("p a b -> p (a b)"),
                                               a, a, ALU.mult)
                    OP("gpsimd", g_sq, reads=["wb1"], writes=["sqb"], tag=f"sq{t}_{u}")
                    if debug and t == NPRE and u == 0:
                        tap("ffn_sq0", sqb[:], ["sqb"])

                    if routed:
                        cbb = bc_ap(c_sb[u % 2], 128, IBLK, CT)

                        def v_h(v, j=j, cbb=cbb):
                            return v.tensor_tensor(h_aps[j],
                                                   sqb[:].rearrange("p a b -> p (a b)"),
                                                   cbb, ALU.mult)
                        OP("vector", v_h, reads=["sqb", ("c", u % 2)],
                           writes=[h_keys[j]], tag=f"h{t}_{u}")
                    else:
                        def v_h(v, j=j):
                            return v.tensor_copy(h_aps[j],
                                                 sqb[:].rearrange("p a b -> p (a b)"))
                        OP("vector", v_h, reads=["sqb"], writes=[h_keys[j]],
                           tag=f"h{t}_{u}")
                for j, u in enumerate(units):
                    idn = 2 + u % 2

                    def mm(tn, j=j, u=u, idn=idn, first=(j == 0), last_u=(j == len(units) - 1)):
                        dnv = wslot[idn][:].rearrange("p (c h) -> p c h", c=ICH)
                        hap = h_aps[j]
                        last = None
                        for hb in range(HB):
                            for ic in range(ICH):
                                wap = dnv[:, ic, hb * 128:(hb + 1) * 128]
                                last = tn.matmul(dn_banks[hb], wap,
                                                 hap[:, ic * CT:(ic + 1) * CT],
                                                 start=(first and ic == 0),
                                                 stop=(last_u and ic == ICH - 1))
                        return last
                    OP("tensor", mm,
                       reads=[h_keys[j], ("wslot", idn)],
                       writes=list(set(dn_keys)), tag=f"dn{t}_{u}")
                    if j + 2 < len(units):
                        down_dma(t, units[j + 2])
                for hb in range(HB):
                    def v_fl(v, hb=hb):
                        return v.tensor_tensor(acc_sb[:, hb, :], acc_sb[:, hb, :],
                                               dn_banks[hb], ALU.add)
                    OP("vector", v_fl, reads=[dn_keys[hb], ("acc", hb)],
                       writes=[("acc", hb)], tag=f"fl{t}_{g0}_{hb}")
                if debug and t == NPRE and g0 == 0:
                    tap("ffn_h0", unA[:], ["unA"])
                    tap("ffn_acc0", acc_sb[:], [("acc", hb2) for hb2 in range(PB)], dt=F32)

        def out_tile(t):
            def f(sy, t=t):
                return sy.dma_start(out=out_hbm[t - NPRE], in_=acc_sb[:])
            OP("sync", f, reads=[("acc", dvb) for dvb in range(PB)], writes=["outdma"],
               dma="dout", tag=f"out{t}")

        def dma_w2(t, src, lo, tag):
            def f0(g):
                return g.dma_start(out=wslot[lo][:], in_=src[:, 0:KC * IM])
            OP("gpsimd", f0, writes=[("wslot", lo)], dma="dw%d" % lo, tag=f"{tag}a{t}")

            def f1(g):
                return g.dma_start(out=wslot[lo + 1][:], in_=src[:, KC * IM:KC * H])
            OP("gpsimd", f1, writes=[("wslot", lo + 1)], dma="dw%d" % (lo + 1),
               tag=f"{tag}b{t}")

        for t in range(NTILE):
            own = t >= NPRE
            rt = t % NW
            dma_w2(t, wk_hbm, 2, "wk")
            load_x(t)
            tab_dma(tabk_hbm[t], f"tk{t}")
            ln_tile(t)
            if debug and t == NPRE:
                tap("xn", xn_sb[:], ["xn"])
            proj_fm(t, 2, kxr[rt], ("kx", rt), "k")
            dma_w2(t, wv_hbm, 2, "wv")
            xpos_apply(t, kxr[rt], ("kx", rt))
            proj_tm_v(t)
            if debug and t == NPRE:
                tap("kx", kxr[rt][:], [("kx", rt)])
                tap("vtm", vtr[rt][:], [("vt", rt)])
            if own:
                dma_w2(t, wq_hbm, 0, "wq")
                proj_fm(t, 0, qx_sb, "qx", "q")
                tab_dma(tabq_hbm[t - NPRE], f"tq{t}")
                xpos_apply(t, qx_sb, "qx")
                if debug and t == NPRE:
                    tap("qx", qx_sb[:], ["qx"])
                attn_tile(t)
                if debug and t == NPRE:
                    tap("acc_attn", acc_sb[:], [("acc", dvb) for dvb in range(PB)],
                        dt=F32)
                routing_tile(t)
                if debug and t == NPRE:
                    tap("comb", rt_m[:], ["rt_m", "rt_mb"], dt=F32)
                up_dma(t, 0)
                ffn_tile(t)
                if debug and t == NPRE:
                    tap("acc_ffn", acc_sb[:], [("acc", dvb) for dvb in range(PB)],
                        dt=F32)
                out_tile(t)

        def f_fin(sy):
            return sy.nop()
        OP("sync", f_fin, reads=["outdma"], tag="final_wait")

        with nc.Block() as block:
            sch.build(block, sems, waitall=("dst",))

    return nc


def _pack_fm(w):
    R, Cc = w.shape
    return np.ascontiguousarray(
        w.reshape(R // 128, 128, Cc).transpose(1, 0, 2).reshape(128, (R // 128) * Cc))


def kernel(L0_attn_X, L0_attn_W_Q, L0_attn_W_K, L0_attn_W_V, L0_ffn_router_w,
           L0_ffn_correction_bias, L0_ffn_up_w, L0_ffn_down_w,
           L0_ffn_shared_up_w, L0_ffn_shared_down_w):
    X = np.asarray(L0_attn_X, dtype=np.float32)
    wq = _pack_fm(np.asarray(L0_attn_W_Q, dtype=np.float32))
    wk = _pack_fm(np.asarray(L0_attn_W_K, dtype=np.float32))
    wv = _pack_fm(np.asarray(L0_attn_W_V, dtype=np.float32))
    router = _pack_fm(np.ascontiguousarray(
        np.asarray(L0_ffn_router_w, dtype=np.float32).T))
    bias = np.ascontiguousarray(
        np.asarray(L0_ffn_correction_bias, dtype=np.float32).reshape(1, E))
    up_w = np.asarray(L0_ffn_up_w, dtype=np.float32)
    down_w = np.asarray(L0_ffn_down_w, dtype=np.float32)
    sup = np.asarray(L0_ffn_shared_up_w, dtype=np.float32)
    sdown = np.asarray(L0_ffn_shared_down_w, dtype=np.float32)

    up_all = np.empty((NUNIT, 128, KC * IM), dtype=np.float32)
    down_all = np.empty((NUNIT, 128, ICH * H), dtype=np.float32)
    for e in range(E):
        up_all[e] = _pack_fm(up_w[e])
        down_all[e] = _pack_fm(down_w[e])
    for s in range(NSH):
        up_all[E + s] = _pack_fm(sup[:, s * IM:(s + 1) * IM])
        down_all[E + s] = _pack_fm(sdown[s * IM:(s + 1) * IM, :])

    nc = build_program()

    import jax
    import jax.numpy as jnp
    cpu = jax.devices("cpu")[0]
    with jax.default_device(cpu):
        dmod = H
        halfd = dmod // 2
        scale_vec = (jnp.arange(0, dmod, 2, dtype=jnp.float32) + 0.4 * dmod) / (1.4 * dmod)
        pos = jnp.arange(L, dtype=jnp.float32)
        scale = scale_vec[None, :] ** (pos[:, None] / SCALE_BASE)
        inv_freq = 1.0 / 10000.0 ** (jnp.arange(halfd, dtype=jnp.float32) / halfd)
        sinusoid = pos[:, None] * inv_freq[None, :]
        sv = jnp.sin(sinusoid)
        cv = jnp.cos(sinusoid)
        sinq = jnp.repeat(sv * scale, 2, axis=-1)
        cosq = jnp.repeat(cv * scale, 2, axis=-1)
        dsc = 1.0 / scale
        sink = jnp.repeat(sv * dsc, 2, axis=-1)
        cosk = jnp.repeat(cv * dsc, 2, axis=-1)
        dlut = GAMMA ** jnp.arange(0, NW * CT, dtype=jnp.float32)
    alt = np.where(np.arange(H) % 2 == 0, -1.0, 1.0).astype(np.float32)
    sinq = np.asarray(sinq) * alt
    cosq = np.asarray(cosq)
    sink = np.asarray(sink) * alt
    cosk = np.asarray(cosk)
    dlut = np.asarray(dlut, dtype=np.float32)

    def _fm_slice(tab, a):
        buf = np.zeros((CT, H), np.float32)
        if a + CT > 0:
            seg = tab[max(a, 0):a + CT]
            buf[CT - seg.shape[0]:] = seg
        return buf.T.reshape(PB, 128, CT).transpose(1, 0, 2).reshape(128, PB * CT)

    p_ = np.arange(128)[:, None, None]
    kb_ = (np.arange(TB) * 128)[None, :, None]
    q_ = np.arange(CT)[None, None, :]
    delta0 = q_ - kb_ - p_
    d0 = np.where(delta0 >= 0, dlut[np.clip(delta0, 0, NW * CT - 1)], 0.0)
    d0 = d0.reshape(128, TB * CT).astype(np.float32)
    dxl = np.empty((128, NW - 1, TB, CT), np.float32)
    for w in range(1, NW):
        dxl[:, w - 1] = dlut[(q_ - kb_ - p_ + w * CT)]
    dxl = dxl.reshape(128, (NW - 1) * TB * CT)

    in_maps = []
    for c in range(NCORE):
        b = c // CORES_PER_SEQ
        s0 = (c % CORES_PER_SEQ) * OWN
        xt = np.zeros((NTILE, 128, PB * CT), dtype=np.float32)
        tabk = np.zeros((NTILE, 128, 2 * PB * CT), dtype=np.float32)
        tabq = np.zeros((NOWN, 128, 2 * PB * CT), dtype=np.float32)
        lo = s0 - PRE
        for t in range(NTILE):
            a = lo + t * CT
            tabk[t, :, 0:PB * CT] = _fm_slice(cosk, a)
            tabk[t, :, PB * CT:] = _fm_slice(sink, a)
            if a + CT <= 0:
                continue
            seg = X[b, max(a, 0):a + CT, :]
            n = seg.shape[0]
            fmv = seg.T.reshape(PB, 128, n).transpose(1, 0, 2)
            buf = np.zeros((128, PB, CT), dtype=np.float32)
            buf[:, :, CT - n:] = fmv
            xt[t] = buf.reshape(128, PB * CT)
        for j in range(NOWN):
            a = s0 + j * CT
            tabq[j, :, 0:PB * CT] = _fm_slice(cosq, a)
            tabq[j, :, PB * CT:] = _fm_slice(sinq, a)
        in_maps.append({
            "x_hbm": xt, "wq_hbm": wq, "wk_hbm": wk, "wv_hbm": wv,
            "router_hbm": router, "bias_hbm": bias, "up_hbm": up_all,
            "down_hbm": down_all, "tabk_hbm": tabk, "tabq_hbm": tabq,
            "d0_hbm": d0, "dx_hbm": dxl,
        })

    res = run_bass_kernel_spmd(nc, in_maps, core_ids=list(range(NCORE)))

    out = np.empty((B, L, H), dtype=np.float32)
    for c in range(NCORE):
        b = c // CORES_PER_SEQ
        s0 = (c % CORES_PER_SEQ) * OWN
        o = np.asarray(res.results[c]["out_hbm"]).reshape(NOWN, 128, PB, CT)
        tm = o.transpose(0, 3, 2, 1).reshape(NOWN * CT, H)
        out[b, s0:s0 + OWN, :] = tm
    return out
